# version 12
# baseline (speedup 1.0000x reference)
import numpy as np
import ml_dtypes
from contextlib import ExitStack

import concourse.bass as bass
import concourse.mybir as mybir
from concourse.bass_utils import run_bass_kernel_spmd

F32 = mybir.dt.float32
BF16 = mybir.dt.bfloat16
AF = mybir.ActivationFunctionType
ALU = mybir.AluOpType
AX = mybir.AxisListType

NEG = -1.0e9
D = 1024
T = 2048
NDC = 8
NW = 4
WQ = 512
NH = 16
HD = 64

ENGS = ("PE", "DVE", "ACT", "POOL", "SP")


def OP(name, *args, **kwargs):
    def b(h):
        return getattr(h, name)(*args, **kwargs)
    return b

BOFF = {"qb": 0, "kvb": 8, "ob": 24, "fc1b": 32, "gm": 272, "sfc1b": 392,
        "sgm": 424, "sfc2b": 440, "battn": 448, "bproj": 472}
NBIAS = 480


class Emitter:
    def __init__(self, nc, ctx):
        self.nc = nc
        self.ctx = ctx
        self.ops = {e: [] for e in ENGS}
        self.cnt = {e: 0 for e in ENGS}
        self.esem = {e: ctx.enter_context(nc.semaphore(f"es_{e}")) for e in ENGS}
        self.dsem = {}
        self.buf = {}
        self.wm = {}
        self.aliasmap = {}

    def set_alias(self, prefixes, pool):
        for p in prefixes:
            self.aliasmap[p] = pool

    def _expand(self, keys):
        alias = []
        for k in keys:
            for pref, pool in self.aliasmap.items():
                if k.startswith(pref):
                    alias.append(pool)
                    break
        return list(keys), alias

    def _dma_sem(self, key):
        if key not in self.dsem:
            self.dsem[key] = [self.ctx.enter_context(self.nc.semaphore(f"ds_{key}")), 0]
        return self.dsem[key]

    def _need(self, waiter, token, waits):
        if token is None:
            return
        if token[0] == "E":
            _, eng, v = token
            pid = ("E", eng)
            sem = self.esem[eng]
        else:
            _, semkey, v = token
            v = self.dsem[semkey][1]
            pid = ("D", semkey)
            sem = self.dsem[semkey][0]
        if self.wm.get((waiter, pid), 0) >= v:
            return
        self.wm[(waiter, pid)] = v
        waits.append((sem, v))

    def _gather(self, waiter, r_role, w_role, w_alias=()):
        waits = []
        for key in r_role:
            rec = self.buf.get(key)
            if rec is not None:
                self._need(waiter, rec["w"], waits)
        for key in w_role:
            rec = self.buf.get(key)
            if rec is not None:
                self._need(waiter, rec["w"], waits)
                for rid, tok in list(rec["r"].items()):
                    self._need(waiter, tok, waits)
        for key in w_alias:
            rec = self.buf.get(key)
            if rec is not None:
                self._need(waiter, rec["w"], waits)
        return waits

    def _record(self, tok, owner, r_all, w_all):
        for key in w_all:
            self.buf[key] = {"w": tok, "r": {}}
        for key in r_all:
            rec = self.buf.setdefault(key, {"w": None, "r": {}})
            rec["r"][owner] = tok

    def emit(self, eng, builder, reads=(), writes=()):
        rr, ra = self._expand(reads)
        wr, wa = self._expand(writes)
        waits = self._gather(eng, rr, wr, wa)
        self.cnt[eng] += 1
        tok = ("E", eng, self.cnt[eng])
        self._record(tok, eng, rr + ra, wr)
        self.ops[eng].append((waits, builder, self.esem[eng], 1))
        return tok

    def dma(self, queue, semkey, out_ap, in_ap, reads=(), writes=()):
        rr, ra = self._expand(reads)
        wr, wa = self._expand(writes)
        waits = self._gather(queue, rr, wr, wa)
        ds = self._dma_sem(semkey)
        ds[1] += 16
        tok = ("D", semkey, ds[1])
        self._record(tok, queue, rr + ra, wr)

        def b(h, o=out_ap, i=in_ap):
            return h.dma_start(out=o, in_=i)

        self.ops[queue].append((waits, b, ds[0], 16))
        return tok

    def dmag(self, queue, semkey, transfers, reads=(), writes=()):
        rr, ra = self._expand(reads)
        wr, wa = self._expand(writes)
        waits = self._gather(queue, rr, wr, wa)
        ds = self._dma_sem(semkey)
        first = True
        for out_ap, in_ap in transfers:
            ds[1] += 16

            def b(h, o=out_ap, i=in_ap):
                return h.dma_start(out=o, in_=i)

            self.ops[queue].append((waits if first else [], b, ds[0], 16))
            first = False
        tok = ("D", semkey, ds[1])
        self._record(tok, queue, rr + ra, wr)
        return tok

    def fence(self, pools, eng="SP"):
        waits = self._gather(eng, [], list(pools), [])
        self.cnt[eng] += 1
        tok = ("E", eng, self.cnt[eng])
        self._record(tok, eng, [], list(pools))
        self.ops[eng].append((waits, lambda h: h.nop(), self.esem[eng], 1))
        return tok

    def finalize_waits(self):
        pairs = [(ds[0], ds[1]) for ds in self.dsem.values()]

        def b(h):
            for sem, val in pairs:
                h.wait_ge(sem, val)
            return h.nop()

        self.ops["SP"].append(([], b, self.esem["SP"], 1))

    def replay(self, block):
        def run(eng):
            def body(h):
                for waits, builder, sem, inc in self.ops[eng]:
                    for s, v in waits:
                        h.wait_ge(s, v)
                    inst = builder(h)
                    if inst is not None:
                        inst.then_inc(sem, inc)
            return body
        block.sync(run("SP"))
        block.gpsimd(run("POOL"))
        block.tensor(run("PE"))
        block.vector(run("DVE"))
        block.scalar(run("ACT"))


def fm(v):
    return np.ascontiguousarray(np.asarray(v, np.float32).reshape(-1, 128).T)


def pair_perm(F2, half):
    F = F2 // 2
    idx = []
    for j in range(F // half):
        idx.extend(range(j * half, (j + 1) * half))
        idx.extend(range(F + j * half, F + (j + 1) * half))
    return np.array(idx)


def host_prepare(inputs):
    f32 = np.float32
    g = {}
    g["l0_qw"] = np.ascontiguousarray(inputs["L0_attn_q_w"], f32)
    g["l0_kvw"] = np.ascontiguousarray(inputs["L0_attn_kv_w"], f32)
    g["l0_ow"] = np.ascontiguousarray(inputs["L0_attn_o_w"], f32)
    perm_m = pair_perm(2048, 512)
    g["l0_fc1w"] = np.ascontiguousarray(
        np.asarray(inputs["L0_ffn_fc1_w"], f32).transpose(0, 2, 1)[:, :, perm_m])
    g["l0_fc2w"] = np.ascontiguousarray(
        np.asarray(inputs["L0_ffn_fc2_w"], f32).transpose(0, 2, 1))
    perm_s = pair_perm(4096, 512)
    g["l0_sfc1w"] = np.ascontiguousarray(np.asarray(inputs["L0_ffn_s_fc1_w"], f32).T[:, perm_s])
    g["l0_sfc2w"] = np.ascontiguousarray(np.asarray(inputs["L0_ffn_s_fc2_w"], f32).T)
    gw = np.zeros((128, NDC * 15), f32)
    gwt = np.asarray(inputs["L0_ffn_gate_w"], f32)
    for dc in range(NDC):
        gw[:, dc * 15:(dc + 1) * 15] = gwt[:, dc * 128:(dc + 1) * 128].T
    g["l0_gatew"] = gw
    g["l0_fc2b"] = np.ascontiguousarray(inputs["L0_ffn_fc2_b"], f32)
    g["l1_wattn"] = np.ascontiguousarray(inputs["L1_attn_W_attn"], f32)
    g["l1_wproj"] = np.ascontiguousarray(inputs["L1_attn_W_proj"], f32)
    g["l1_temp"] = np.asarray(inputs["L1_attn_temperature"], f32).reshape(1, 1)
    g["l1_wg"] = np.ascontiguousarray(inputs["L1_ffn_Wg"], f32)
    g["l1_wu"] = np.ascontiguousarray(inputs["L1_ffn_Wu"], f32)
    g["l1_wd"] = np.ascontiguousarray(inputs["L1_ffn_Wd"], f32)
    bfm = np.zeros((128, NBIAS), f32)
    bfm[:, 0:8] = fm(inputs["L0_attn_q_b"])
    bfm[:, 8:24] = fm(inputs["L0_attn_kv_b"])
    bfm[:, 24:32] = fm(inputs["L0_attn_o_b"])
    fc1b_p = np.asarray(inputs["L0_ffn_fc1_b"], f32)[:, perm_m]
    for e in range(15):
        bfm[:, 32 + e * 16:48 + e * 16] = fm(fc1b_p[e])
        bfm[:, 272 + e * 8:280 + e * 8] = fm(np.asarray(inputs["L0_ffn_geglu_mult"], f32)[e])
    bfm[:, 392:424] = fm(np.asarray(inputs["L0_ffn_s_fc1_b"], f32)[perm_s])
    bfm[:, 424:440] = fm(inputs["L0_ffn_s_geglu_mult"])
    bfm[:, 440:448] = fm(inputs["L0_ffn_s_fc2_b"])
    bfm[:, 448:472] = fm(inputs["L1_attn_b_attn"])
    bfm[:, 472:480] = fm(inputs["L1_attn_b_proj"])
    g["b_fm"] = bfm
    g["ident"] = np.eye(128, dtype=f32)
    i = np.arange(128)
    g["cmask"] = np.where(i[:, None] <= i[None, :], 0.0, NEG).astype(f32)
    ak = np.zeros((2, 16, 4, T), ml_dtypes.bfloat16)
    kidx = np.arange(T, dtype=np.float64)
    for h in range(16):
        for li, s in enumerate(((h + 1) / 16.0, 2.0 ** (-(h + 1) / 2.0))):
            v = s * kidx
            hi = v.astype(ml_dtypes.bfloat16)
            r1 = v - hi.astype(np.float64)
            mid = r1.astype(ml_dtypes.bfloat16)
            lo = (r1 - mid.astype(np.float64)).astype(ml_dtypes.bfloat16)
            ak[li, h, 0], ak[li, h, 1], ak[li, h, 2] = hi, mid, lo
            ak[li, h, 3] = v.astype(ml_dtypes.bfloat16)
    g["alibik"] = ak

    xs = np.asarray(inputs["L0_attn_x"], f32)
    per_core_x = [np.ascontiguousarray(xs[2 * c:2 * c + 2].transpose(0, 2, 1))
                  for c in range(8)]
    return g, per_core_x


class Builder:
    def __init__(self, n_batches=2, stop_after=None, dbg=()):
        self.n_batches = n_batches
        self.stop_after = stop_after
        self.dbg = set(dbg)
        self.nc = bass.Bass("TRN2", target_bir_lowering=False, debug=False)

    def mm_group(self, bank, psum_ap, pairs, reads, extra_writes=(), acc_cont=False,
                 group_end=True):
        def b(h, pairs=tuple(pairs), psum_ap=psum_ap):
            inst = None
            n = len(pairs)
            for i, (lhsT, rhs) in enumerate(pairs):
                inst = h.matmul(psum_ap, lhsT, rhs,
                                start=(i == 0 and not acc_cont),
                                stop=(i == n - 1 and group_end))
            return inst
        return self.em.emit("PE", b, reads=reads, writes=[bank, *extra_writes])

    def mm_group2(self, bankA, apA, pairsA, bankB, apB, pairsB, reads):
        def b(h, pa=tuple(pairsA), pb=tuple(pairsB), apA=apA, apB=apB):
            n = len(pa)
            for i, (lhsT, rhs) in enumerate(pa):
                h.matmul(apA, lhsT, rhs, start=(i == 0), stop=(i == n - 1))
            m = len(pb)
            inst = None
            for i, (lhsT, rhs) in enumerate(pb):
                inst = h.matmul(apB, lhsT, rhs, start=(i == 0), stop=(i == m - 1))
            return inst
        return self.em.emit("PE", b, reads=reads, writes=[bankA, bankB])

    def bias(self, name, idx):
        return self.bfm[:, BOFF[name] + idx:BOFF[name] + idx + 1]

    def build(self):
        nc = self.nc
        self.d_in = {}

        def din(name, shape, dt=F32):
            self.d_in[name] = nc.dram_tensor(name, list(shape), dt, kind="ExternalInput").ap()
            return self.d_in[name]

        self.x_d = din("x", (2, 1024, T))
        din("l0_qw", (1024, 1024)); din("l0_kvw", (1024, 2048)); din("l0_ow", (1024, 1024))
        din("l0_fc1w", (15, 1024, 2048)); din("l0_fc2w", (15, 1024, 1024))
        din("l0_sfc1w", (1024, 4096)); din("l0_sfc2w", (2048, 1024))
        din("l0_gatew", (128, NDC * 15)); din("l0_fc2b", (15, 1024))
        din("l1_wattn", (1024, 3072)); din("l1_wproj", (1024, 1024)); din("l1_temp", (1, 1))
        din("l1_wg", (8, 1024, 4096)); din("l1_wu", (8, 1024, 4096)); din("l1_wd", (8, 4096, 1024))
        din("b_fm", (128, NBIAS)); din("ident", (128, 128)); din("cmask", (128, 128))
        din("alibik", (2, 16, 4, T), BF16)
        self.out_d = nc.dram_tensor("out", [2, 1024, T], F32, kind="ExternalOutput").ap()
        self.x1scr = nc.dram_tensor("x1scr", [2, 1024, T], F32, kind="Internal").ap()
        self.dbg_d = {}
        if "xn" in self.dbg:
            self.dbg_d["dbg_xn"] = nc.dram_tensor("dbg_xn", [self.n_batches, 128, 16384], F32, kind="ExternalOutput").ap()
        if "comb" in self.dbg:
            self.dbg_d["dbg_comb"] = nc.dram_tensor("dbg_comb", [self.n_batches, 128, 240], F32, kind="ExternalOutput").ap()
        if "acc" in self.dbg:
            self.dbg_d["dbg_acc"] = nc.dram_tensor("dbg_acc", [self.n_batches, 128, 16384], F32, kind="ExternalOutput").ap()
        if "yT" in self.dbg:
            self.dbg_d["dbg_yT"] = nc.dram_tensor("dbg_yT", [self.n_batches, 128, 16384], F32, kind="ExternalOutput").ap()
        if "x1" in self.dbg:
            self.dbg_d["dbg_x1"] = nc.dram_tensor("dbg_x1", [self.n_batches, 128, 16384], F32, kind="ExternalOutput").ap()

        ctx = ExitStack()
        with ctx:
            sbt = lambda n, s, dt: ctx.enter_context(nc.sbuf_tensor(n, list(s), dt))
            self.xn = sbt("xn", (128, 16384), BF16)
            self.acc = sbt("acc", (128, 16384), F32)
            self.pA = sbt("pA", (128, 16384), BF16)
            self.pB = sbt("pB", (128, 8192), BF16)
            self.pC = sbt("pC", (128, 12352), BF16)
            self.sc = [sbt(f"sc{i}", (128, 512), F32) for i in range(4)]
            self.tb = [sbt(f"tb{i}", (128, 512), BF16) for i in range(2)]
            self.rowA = sbt("rowA", (1, 512), F32)
            self.rowB = sbt("rowB", (1, 512), F32)
            self.rowM = sbt("rowM", (1, 512), F32)
            self.rowS = sbt("rowS", (1, 16), F32)
            self.rowSLP = sbt("rowSLP", (1, T), BF16)
            self.invcol = sbt("invcol", (128, 1), F32)
            self.comb = sbt("comb", (128, 16 * 15), F32)
            self.combT = sbt("combT", (15, T), F32)
            self.gatew = sbt("gatew", (128, NDC * 15), F32)
            self.bfm = sbt("bfm", (128, NBIAS), F32)
            self.fc2b = sbt("fc2b", (15, 1024), F32)
            self.identt = sbt("identt", (128, 128), F32)
            self.cmaskt = sbt("cmaskt", (128, 128), F32)
            self.onesC = sbt("onesC", (128, 1), F32)
            self.onesCb = sbt("onesCb", (128, 1), BF16)
            self.onesRb = sbt("onesRb", (1, 128), BF16)
            self.rowAb = sbt("rowAb", (1, 512), BF16)
            self.rowBb = sbt("rowBb", (1, 512), BF16)
            self.onesR = sbt("onesR", (1, 128), F32)
            self.psq01 = ctx.enter_context(nc.psum_tensor("psq01", [128, 1024], F32))
            self.psq23 = ctx.enter_context(nc.psum_tensor("psq23", [128, 1024], F32))
            self.ps = [self.psq01[:, 0:512], self.psq01[:, 512:1024],
                       self.psq23[:, 0:512], self.psq23[:, 512:1024]] + \
                      [ctx.enter_context(nc.psum_tensor(f"ps{i}", [128, 512], F32))
                       for i in range(4, 8)]
            self.em = Emitter(nc, ctx)
            self.em.set_alias(("wA", "yT_", "wg", "wu"), "pAx")
            self.em.set_alias(("fc2s", "abufM", "pw", "ow", "mbuf"), "pBx")
            self.em.set_alias(("fc2m", "vp", "kaug", "qaug", "pt", "wd"), "pCx")
            self.program()
            self.em.finalize_waits()
            with nc.allow_low_precision(reason="bf16 broadcast-source rows/stats; deliberate, within tolerance"):
                with nc.Block() as block:
                    self.em.replay(block)
        return self

    def program(self):
        em, d = self.em, self.d_in
        em.dmag("SP", "const", [
            (self.gatew[:], d["l0_gatew"][:]),
            (self.bfm[:], d["b_fm"][:]),
            (self.fc2b[:], d["l0_fc2b"][:]),
            (self.identt[:], d["ident"][:]),
            (self.cmaskt[:], d["cmask"][:]),
            (self.rowS[0:1, 4:5], d["l1_temp"][:]),
        ], writes=["gatew", "bfm", "fc2b", "identt", "cmaskt", "rowS"])
        em.emit("DVE", OP("memset", self.onesC[:], 1.0), writes=["onesC"])
        em.emit("DVE", OP("memset", self.onesCb[:], 1.0), writes=["onesCb"])
        em.emit("DVE", OP("memset", self.onesRb[:], 1.0), writes=["onesRb"])
        em.emit("DVE", OP("memset", self.onesR[:], 1.0), writes=["onesR"])
        for b in range(self.n_batches):
            self.phase_A(b)
            if self.stop_after == "A":
                continue
            self.phase_B1(b)
            if self.stop_after == "B1":
                continue
            self.phase_B2(b)
            if self.stop_after == "B2":
                continue
            self.phase_attn(b, 0)
            if self.stop_after == "C":
                continue
            self.phase_D(b)
            if self.stop_after == "D":
                continue
            self.phase_attn(b, 1)
        if "comb" in self.dbg:
            for b in range(self.n_batches):
                pass

    def acck(self, dt, w):
        return f"acc_{dt}_{w}"

    def phase_A(self, b):
        em, E, DMA = self.em, self.em.emit, self.em.dma
        ps, sc, acc, xn, tb = self.ps, self.sc, self.acc, self.xn, self.tb
        em.dmag("SP", "xload",
                [(acc[:, dc * 2048:(dc + 1) * 2048], self.x_d[b, dc * 128:(dc + 1) * 128, :])
                 for dc in range(NDC)],
                writes=[self.acck(dc, w) for dc in range(NDC) for w in range(NW)])
        for w in range(NW):
            wsl = slice(w * WQ, (w + 1) * WQ)
            for dc in range(NDC):
                ac = acc[:, dc * 2048 + w * WQ: dc * 2048 + (w + 1) * WQ]
                tcp = tb[dc % 2]
                self.mm_group("ps6", ps[6][0:1, :], [(self.onesC[:], ac)],
                              reads=[self.acck(dc, w), "onesC"], acc_cont=(dc > 0),
                              group_end=(dc == NDC - 1))
                E("ACT", OP("square", tcp[:], ac), reads=[self.acck(dc, w)], writes=[f"tb{dc % 2}"])
                self.mm_group("ps7", ps[7][0:1, :], [(self.onesCb[:], tcp[:])],
                              reads=[f"tb{dc % 2}", "onesCb"], acc_cont=(dc > 0),
                              group_end=(dc == NDC - 1))
            E("DVE", OP("tensor_scalar_mul", self.rowAb[:], ps[6][0:1, :], 1.0 / D),
              reads=["ps6"], writes=["rowAb"])
            E("DVE", OP("tensor_scalar_mul", self.rowB[:], ps[7][0:1, :], 1.0 / D),
              reads=["ps7"], writes=["rowB"])
            E("DVE", OP("tensor_tensor", self.rowM[:], self.rowAb[:], self.rowAb[:], op=ALU.mult),
              reads=["rowAb"], writes=["rowM"])
            E("DVE", OP("tensor_tensor", self.rowB[:], self.rowB[:], self.rowM[:], op=ALU.subtract),
              reads=["rowB", "rowM"], writes=["rowB"])
            E("DVE", OP("tensor_scalar_add", self.rowB[:], self.rowB[:], 1.0e-5),
              reads=["rowB"], writes=["rowB"])
            E("ACT", OP("activation", self.rowM[:], self.rowB[:], AF.Sqrt),
              reads=["rowB"], writes=["rowM"])
            E("DVE", OP("reciprocal", self.rowBb[:], self.rowM[:]),
              reads=["rowM"], writes=["rowBb"])
            self.mm_group("ps4", ps[4][:, :], [(self.onesRb[:], self.rowAb[:])],
                          reads=["rowAb", "onesRb"], extra_writes=[])
            self.mm_group("ps5", ps[5][:, :], [(self.onesRb[:], self.rowBb[:])],
                          reads=["rowBb", "onesRb"])
            for dc in range(NDC):
                ac = acc[:, dc * 2048 + w * WQ: dc * 2048 + (w + 1) * WQ]
                sx = sc[dc % 2]
                E("DVE", OP("tensor_tensor", sc[2][:], ac, ps[4][:, :], op=ALU.subtract),
                  reads=[self.acck(dc, w), "ps4"], writes=["sc2"])
                E("DVE", OP("tensor_tensor", sx[:], sc[2][:], ps[5][:, :], op=ALU.mult),
                  reads=["sc2", "ps5"], writes=[f"sc{dc % 2}"])
                E("DVE", OP("tensor_copy", 
                    self.xn[:, dc * 2048 + w * WQ: dc * 2048 + (w + 1) * WQ], sx[:]),
                  reads=[f"sc{dc % 2}"], writes=[f"xn_{dc}_{w}"])
                for tt in range(4):
                    E("PE", OP("matmul", 
                        ps[tt][0:128, 0:15], sx[:, tt * 128:(tt + 1) * 128],
                        self.gatew[:, dc * 15:(dc + 1) * 15],
                        start=(dc == 0), stop=(dc == NDC - 1)),
                      reads=[f"sc{dc % 2}", "gatew"], writes=[f"ps{tt}"])
            for tt in range(4):
                g = ps[tt][0:128, 0:15]
                s3 = sc[3]
                E("DVE", OP("tensor_reduce", s3[:, 0:1], g, axis=AX.X, op=ALU.max, negate=True),
                  reads=[f"ps{tt}"], writes=["sc3"])
                E("ACT", OP("activation", s3[:, 1:16], g, AF.Exp, bias=s3[:, 0:1]),
                  reads=[f"ps{tt}", "sc3"], writes=["sc3"])
                E("DVE", OP("tensor_reduce", s3[:, 16:17], s3[:, 1:16], axis=AX.X, op=ALU.add),
                  reads=["sc3"], writes=["sc3"])
                E("DVE", OP("reciprocal", s3[:, 17:18], s3[:, 16:17]),
                  reads=["sc3"], writes=["sc3"])
                E("DVE", OP("tensor_scalar", s3[:, 32:47], s3[:, 1:16], s3[:, 17:18], None, op0=ALU.mult),
                  reads=["sc3"], writes=["sc3"])
                E("DVE", OP("max", out=s3[:, 48:56], in_=s3[:, 32:47]),
                  reads=["sc3"], writes=["sc3"])
                E("DVE", OP("tensor_reduce", s3[:, 56:57], s3[:, 48:51], axis=AX.X, op=ALU.add),
                  reads=["sc3"], writes=["sc3"])
                E("DVE", OP("tensor_scalar_add", s3[:, 56:57], s3[:, 56:57], 1.0e-20),
                  reads=["sc3"], writes=["sc3"])
                E("DVE", OP("reciprocal", s3[:, 57:58], s3[:, 56:57]),
                  reads=["sc3"], writes=["sc3"])
                E("DVE", OP("tensor_scalar", s3[:, 64:79], s3[:, 32:47], s3[:, 50:51], None, op0=ALU.is_ge),
                  reads=["sc3"], writes=["sc3"])
                E("DVE", OP("tensor_scalar", s3[:, 80:95], s3[:, 32:47], s3[:, 57:58], None, op0=ALU.mult),
                  reads=["sc3"], writes=["sc3"])
                ci = (w * 4 + tt) * 15
                E("DVE", OP("tensor_tensor", 
                    self.comb[:, ci:ci + 15], s3[:, 64:79], s3[:, 80:95], op=ALU.mult),
                  reads=["sc3"], writes=["comb"])
                E("PE", OP("transpose", 
                    ps[5][0:15, 0:128], self.comb[:, ci:ci + 15], self.identt[:]),
                  reads=["comb", "identt"], writes=["ps5"])
                cj = (w * 4 + tt) * 128
                E("DVE", OP("tensor_copy", self.combT[0:15, cj:cj + 128], ps[5][0:15, 0:128]),
                  reads=["ps5"], writes=["combT"])
        if "dbg_xn" in self.dbg_d and self.stop_after == "A":
            DMA("POOL", "dbgxn", self.dbg_d["dbg_xn"][b], self.xn[:],
                reads=[f"xn_{dc}_{w}" for dc in range(NDC) for w in range(NW)])
        if "dbg_comb" in self.dbg_d:
            DMA("SP", "dbgcomb", self.dbg_d["dbg_comb"][b], self.comb[:], reads=["comb"])

    def phase_B1(self, b):
        em, E, DMA = self.em, self.em.emit, self.em.dma
        em.fence(["pAx", "pBx", "pCx"])
        ps, sc, tb, acc, xn, pA, pB, pC = self.ps, self.sc, self.tb, self.acc, self.xn, self.pA, self.pB, self.pC
        d = self.d_in
        for j in range(4):
            wk = f"wA{j % 2}"
            wbase = (j % 2) * 8192
            em.dmag("POOL", wk,
                    [(pA[:, wbase + dc * 1024: wbase + (dc + 1) * 1024],
                      d["l0_sfc1w"][dc * 128:(dc + 1) * 128, j * 1024:(j + 1) * 1024])
                     for dc in range(NDC)], writes=[wk])
            fk = f"fc2s{j % 2}"
            fbase = (j % 2) * 4096
            em.dmag("POOL", fk,
                    [(pB[:, fbase + fc * 1024: fbase + (fc + 1) * 1024],
                      d["l0_sfc2w"][j * 512 + fc * 128: j * 512 + (fc + 1) * 128, :])
                     for fc in range(4)], writes=[fk])
            for w in range(NW):
                wsl = slice(w * WQ, (w + 1) * WQ)
                abase = 8192 + (w % 2) * 2048
                ak = f"abufS{w % 2}"
                for u in range(4):
                    bx = 2 * (u % 2)
                    self.mm_group(f"ps{bx}", ps[bx][:, :],
                                  [(pA[:, wbase + dc * 1024 + u * 128: wbase + dc * 1024 + (u + 1) * 128],
                                    xn[:, dc * 2048 + w * WQ: dc * 2048 + (w + 1) * WQ])
                                   for dc in range(NDC)],
                                  reads=[wk] + [f"xn_{dc}_{w}" for dc in range(NDC)])
                    self.mm_group(f"ps{bx + 1}", ps[bx + 1][:, :],
                                  [(pA[:, wbase + dc * 1024 + 512 + u * 128: wbase + dc * 1024 + 512 + (u + 1) * 128],
                                    xn[:, dc * 2048 + w * WQ: dc * 2048 + (w + 1) * WQ])
                                   for dc in range(NDC)],
                                  reads=[wk] + [f"xn_{dc}_{w}" for dc in range(NDC)])
                    bias_g = self.bias("sfc1b", j * 8 + 4 + u)
                    bias_x = self.bias("sfc1b", j * 8 + u)
                    gmcol = self.bias("sgm", j * 4 + u)
                    E("ACT", OP("activation", 
                        tb[u % 2][:], ps[bx + 1][:, :], AF.Gelu, bias=bias_g),
                      reads=[f"ps{bx + 1}", "bfm"], writes=[f"tb{u % 2}"])
                    E("DVE", OP("tensor_scalar", 
                        sc[2][:], ps[bx][:, :], bias_x, gmcol, op0=ALU.add, op1=ALU.mult),
                      reads=[f"ps{bx}", "bfm"], writes=["sc2"])
                    E("DVE", OP("tensor_tensor", 
                        pC[:, abase + u * 512: abase + (u + 1) * 512], sc[2][:], tb[u % 2][:], op=ALU.mult),
                      reads=["sc2", f"tb{u % 2}"], writes=[ak])
                for dt in range(NDC):
                    by = 4 + (dt % 2)
                    self.mm_group(f"ps{by}", ps[by][:, :],
                                  [(pB[:, fbase + fc * 1024 + dt * 128: fbase + fc * 1024 + (dt + 1) * 128],
                                    pC[:, abase + fc * 512: abase + (fc + 1) * 512])
                                   for fc in range(4)],
                                  reads=[fk, ak])
                    asl = (slice(None), slice(dt * 2048 + w * WQ, dt * 2048 + (w + 1) * WQ))
                    if j == 0:
                        bcol = self.bias("sfc2b", dt)
                        E("DVE", OP("tensor_scalar", 
                            acc[asl], ps[by][:, :], bcol, None, op0=ALU.add),
                          reads=[f"ps{by}", "bfm"], writes=[self.acck(dt, w)])
                    else:
                        E("DVE", OP("tensor_tensor", 
                            acc[asl], acc[asl], ps[by][:, :], op=ALU.add),
                          reads=[f"ps{by}", self.acck(dt, w)], writes=[self.acck(dt, w)])
        if self.stop_after == "B1" and "dbg_acc" in self.dbg_d:
            DMA("SP", "dbgacc", self.dbg_d["dbg_acc"][b], acc[:],
                reads=[self.acck(dt, w) for dt in range(NDC) for w in range(NW)])

    def phase_B2(self, b):
        em, E, DMA = self.em, self.em.emit, self.em.dma
        em.fence(["pBx", "pCx"])
        ps, sc, tb, acc, xn, pA, pB, pC = self.ps, self.sc, self.tb, self.acc, self.xn, self.pA, self.pB, self.pC
        d = self.d_in
        for e in range(15):
            for half in range(2):
                slot = (e * 2 + half) % 3
                em.dmag("POOL", f"fc2m{slot}",
                        [(pC[:, slot * 4096 + fc * 1024: slot * 4096 + (fc + 1) * 1024],
                          d["l0_fc2w"][e, (half * 4 + fc) * 128:(half * 4 + fc + 1) * 128, :])
                         for fc in range(4)], writes=[f"fc2m{slot}"])
            for j in range(2):
                wk = f"wA{j}"
                wbase = j * 8192
                em.dmag("POOL", wk,
                        [(pA[:, wbase + dc * 1024: wbase + (dc + 1) * 1024],
                          d["l0_fc1w"][e, dc * 128:(dc + 1) * 128, j * 1024:(j + 1) * 1024])
                         for dc in range(NDC)], writes=[wk])
            for j in range(2):
                wk = f"wA{j}"
                wbase = j * 8192
                for w in range(NW):
                    em.dma("POOL", "rowcb", self.rowAb[0:1, :], self.combT[e:e + 1, w * WQ:(w + 1) * WQ],
                           reads=["combT"], writes=["rowAb"])
                    self.mm_group("ps6", ps[6][:, :],
                                  [(self.onesRb[:], self.rowAb[0:1, :])],
                                  reads=["rowAb", "onesRb"])
                    E("DVE", OP("tensor_copy", sc[3][:], ps[6][:, :]),
                      reads=["ps6"], writes=["sc3"])
                    for u in range(4):
                        fcg = j * 4 + u
                        bx = 2 * (u % 2)
                        self.mm_group(f"ps{bx}", ps[bx][:, :],
                                      [(pA[:, wbase + dc * 1024 + u * 128: wbase + dc * 1024 + (u + 1) * 128],
                                        xn[:, dc * 2048 + w * WQ: dc * 2048 + (w + 1) * WQ])
                                       for dc in range(NDC)],
                                      reads=[wk] + [f"xn_{dc}_{w}" for dc in range(NDC)])
                        self.mm_group(f"ps{bx + 1}", ps[bx + 1][:, :],
                                      [(pA[:, wbase + dc * 1024 + 512 + u * 128: wbase + dc * 1024 + 512 + (u + 1) * 128],
                                        xn[:, dc * 2048 + w * WQ: dc * 2048 + (w + 1) * WQ])
                                       for dc in range(NDC)],
                                      reads=[wk] + [f"xn_{dc}_{w}" for dc in range(NDC)])
                        bias_g = self.bias("fc1b", e * 16 + j * 8 + 4 + u)
                        bias_x = self.bias("fc1b", e * 16 + j * 8 + u)
                        gmcol = self.bias("gm", e * 8 + fcg)
                        E("ACT", OP("activation", 
                            tb[u % 2][:], ps[bx + 1][:, :], AF.Gelu, bias=bias_g),
                          reads=[f"ps{bx + 1}", "bfm"], writes=[f"tb{u % 2}"])
                        sx = sc[u % 2]
                        E("DVE", OP("tensor_scalar", 
                            sx[:], ps[bx][:, :], bias_x, gmcol, op0=ALU.add, op1=ALU.mult),
                          reads=[f"ps{bx}", "bfm"], writes=[f"sc{u % 2}"])
                        E("DVE", OP("tensor_tensor", sx[:], sx[:], tb[u % 2][:], op=ALU.mult),
                          reads=[f"sc{u % 2}", f"tb{u % 2}"], writes=[f"sc{u % 2}"])
                        ac = (u * 4 + w) * 512
                        E("DVE", OP("tensor_tensor", 
                            pB[:, ac:ac + 512], sx[:], sc[3][:], op=ALU.mult),
                          reads=[f"sc{u % 2}", "sc3"], writes=[f"abufM_{w}"])
                slot = (e * 2 + j) % 3
                for w in range(NW):
                    for dt in range(NDC):
                        by = 4 + (dt % 2)
                        self.mm_group(f"ps{by}", ps[by][:, :],
                                      [(pC[:, slot * 4096 + fc * 1024 + dt * 128: slot * 4096 + fc * 1024 + (dt + 1) * 128],
                                        pB[:, (fc * 4 + w) * 512: (fc * 4 + w) * 512 + 512])
                                       for fc in range(4)],
                                      reads=[f"fc2m{slot}", f"abufM_{w}"])
                        asl = (slice(None), slice(dt * 2048 + w * WQ, dt * 2048 + (w + 1) * WQ))
                        E("DVE", OP("tensor_tensor", 
                            acc[asl], acc[asl], ps[by][:, :], op=ALU.add),
                          reads=[f"ps{by}", self.acck(dt, w)], writes=[self.acck(dt, w)])
        for w in range(NW):
            for dt in range(NDC):
                by = 4 + (dt % 2)
                self.mm_group(f"ps{by}", ps[by][:, :],
                              [(self.fc2b[:, dt * 128:(dt + 1) * 128],
                                self.combT[:, w * WQ:(w + 1) * WQ])],
                              reads=["fc2b", "combT"])
                asl = (slice(None), slice(dt * 2048 + w * WQ, dt * 2048 + (w + 1) * WQ))
                E("DVE", OP("tensor_tensor", 
                    acc[asl], acc[asl], ps[by][:, :], op=ALU.add),
                  reads=[f"ps{by}", self.acck(dt, w)], writes=[self.acck(dt, w)])
        if self.stop_after == "B2" and "dbg_acc" in self.dbg_d:
            DMA("SP", "dbgacc", self.dbg_d["dbg_acc"][b], acc[:],
                reads=[self.acck(dt, w) for dt in range(NDC) for w in range(NW)])

    def phase_attn(self, b, L):
        em, E, DMA = self.em, self.em.emit, self.em.dma
        em.fence(["pAx", "pBx", "pCx"])
        ps, sc, tb, acc, xn, pA, pB, pC = self.ps, self.sc, self.tb, self.acc, self.xn, self.pA, self.pB, self.pC
        d = self.d_in
        VP = 0
        KAx = (2112, 4160)
        QAx = (6208, 8256)
        PTb = 10304
        SB = (0, 1, 4, 5)
        DEPTH = 3
        if L == 0:
            wsrc, koff, voff = d["l0_kvw"], 0, 1024
            qwsrc = d["l0_qw"]
            qb, kb, vb = ("qb", 0), ("kvb", 0), ("kvb", 8)
            owsrc, ob = d["l0_ow"], "ob"
        else:
            wsrc, koff, voff = d["l1_wattn"], 1024, 2048
            qwsrc = d["l1_wattn"]
            qb, kb, vb = ("battn", 0), ("battn", 8), ("battn", 16)
            owsrc, ob = d["l1_wproj"], "bproj"
        if L == 0:
            E("DVE", OP("memset", self.invcol[:], 0.125), writes=["invcol"])
        else:
            E("DVE", OP("tensor_scalar_mul", self.rowS[0:1, 5:6], self.rowS[0:1, 4:5], 8.0),
              reads=["rowS"], writes=["rowS"])
            E("DVE", OP("reciprocal", self.rowS[0:1, 6:7], self.rowS[0:1, 5:6]),
              reads=["rowS"], writes=["rowS"])
            self.mm_group("ps4", ps[4][0:128, 0:1], [(self.onesR[:], self.rowS[0:1, 6:7])],
                          reads=["rowS", "onesR"])
            E("DVE", OP("tensor_copy", self.invcol[:], ps[4][:, 0:1]),
              reads=["ps4"], writes=["invcol"])

        for hl in range(2):
            E("DVE", OP("memset", pC[64:65, KAx[hl]:KAx[hl] + 2048], 1.0),
              writes=[f"kaug{hl}"])
            E("DVE", OP("memset", pC[64:68, QAx[hl]:QAx[hl] + 2048], 1.0),
              writes=[f"qaug{hl}"])
        E("DVE", OP("memset", pC[:, VP:VP + 16 * 132], 1.0), writes=["vp"])
        for p in range(8):
            s = p % 2
            base = s * 3072
            wkq = f"pw{s}"
            xfers = []
            for dc in range(NDC):
                xfers.append((pB[:, base + dc * 128: base + (dc + 1) * 128],
                              qwsrc[dc * 128:(dc + 1) * 128, p * 128:(p + 1) * 128]))
            for dc in range(NDC):
                xfers.append((pB[:, base + 1024 + dc * 128: base + 1024 + (dc + 1) * 128],
                              wsrc[dc * 128:(dc + 1) * 128, koff + p * 128: koff + (p + 1) * 128]))
            for dc in range(NDC):
                xfers.append((pB[:, base + 2048 + dc * 128: base + 2048 + (dc + 1) * 128],
                              wsrc[dc * 128:(dc + 1) * 128, voff + p * 128: voff + (p + 1) * 128]))
            em.dmag("POOL", wkq, xfers, writes=[wkq])
            for hl in range(2):
                DMA("SP", f"alibik{hl}", pC[65:68, KAx[hl]:KAx[hl] + 2048],
                    d["alibik"][L, 2 * p + hl, 0:3], writes=[f"kaug{hl}"])

            vbcol = self.bias(vb[0], vb[1] + p)
            E("PE", OP("transpose", ps[4][0:1, 0:128], vbcol, self.identt[:]),
              reads=["bfm", "identt"], writes=["ps4"])
            E("DVE", OP("tensor_copy", self.rowAb[0:1, 0:128], ps[4][0:1, 0:128]),
              reads=["ps4"], writes=["rowAb"])
            self.mm_group("ps4", ps[4][0:128, 0:128], [(self.onesRb[:], self.rowAb[0:1, 0:128])],
                          reads=["rowAb", "onesRb"])
            E("DVE", OP("tensor_copy", sc[3][:, 0:128], ps[4][:, 0:128]),
              reads=["ps4"], writes=["sc3"])
            qbc, kbc = self.bias(qb[0], qb[1] + p), self.bias(kb[0], kb[1] + p)
            for w in range(NW):
                bq = 4 + w % 2
                wsl = slice(w * WQ, (w + 1) * WQ)
                self.mm_group(f"ps{bq}", ps[bq][:, :],
                              [(pB[:, base + dc * 128: base + (dc + 1) * 128],
                                xn[:, dc * 2048 + w * WQ: dc * 2048 + (w + 1) * WQ])
                               for dc in range(NDC)],
                              reads=[wkq] + [f"xn_{dc}_{w}" for dc in range(NDC)])
                for hl in range(2):
                    r0 = hl * 64
                    E("DVE", OP("tensor_scalar", pC[0:64, QAx[hl] + w * WQ: QAx[hl] + (w + 1) * WQ],
                                ps[bq][r0:r0 + 64, :], qbc[r0:r0 + 64, :], self.invcol[r0:r0 + 64, 0:1],
                                op0=ALU.add, op1=ALU.mult),
                      reads=[f"ps{bq}", "bfm", "invcol"], writes=[f"qaug{hl}"])
                self.mm_group(f"ps{2 + bq}", ps[2 + bq][:, :],
                              [(pB[:, base + 1024 + dc * 128: base + 1024 + (dc + 1) * 128],
                                xn[:, dc * 2048 + w * WQ: dc * 2048 + (w + 1) * WQ])
                               for dc in range(NDC)],
                              reads=[wkq] + [f"xn_{dc}_{w}" for dc in range(NDC)])
                for hl in range(2):
                    r0 = hl * 64
                    E("DVE", OP("tensor_scalar", pC[0:64, KAx[hl] + w * WQ: KAx[hl] + (w + 1) * WQ],
                                ps[2 + bq][r0:r0 + 64, :], kbc[r0:r0 + 64, :], None, op0=ALU.add),
                      reads=[f"ps{2 + bq}", "bfm"], writes=[f"kaug{hl}"])
            for hl in range(2):
                E("DVE", OP("memset", self.rowS[0:1, 0 + 2 * hl: 1 + 2 * hl], 0.0), writes=["rowS"])
                E("DVE", OP("memset", self.rowS[0:1, 8 + hl: 9 + hl], 0.0), writes=["rowS"])
            for w in range(NW):
                for which in ("k", "q"):
                    for hl in range(2):
                        cell = (self.rowS[0:1, 0 + 2 * hl: 1 + 2 * hl] if which == "k"
                                else self.rowS[0:1, 8 + hl: 9 + hl])
                        buf = KAx[hl] if which == "k" else QAx[hl]
                        tbq = self.tb[hl]
                        E("ACT", OP("square", tbq[:64, :], pC[0:64, buf + w * WQ: buf + (w + 1) * WQ]),
                          reads=[f"kaug{hl}" if which == "k" else f"qaug{hl}"], writes=[f"tb{hl}"])
                        self.mm_group(f"ps{6 + hl}", ps[6 + hl][0:1, :], [(self.onesCb[0:64, :], tbq[:64, :])],
                                      reads=[f"tb{hl}", "onesCb"])
                        rw = self.rowA if hl == 0 else self.rowB
                        rk = "rowA" if hl == 0 else "rowB"
                        E("DVE", OP("tensor_reduce", rw[0:1, 0:1], ps[6 + hl][0:1, :], axis=AX.X, op=ALU.max),
                          reads=[f"ps{6 + hl}"], writes=[rk])
                        E("DVE", OP("tensor_tensor", cell, cell, rw[0:1, 0:1], op=ALU.max),
                          reads=[rk, "rowS"], writes=["rowS"])
            for hl in range(2):
                kcell = self.rowS[0:1, 0 + 2 * hl: 1 + 2 * hl]
                qcell = self.rowS[0:1, 8 + hl: 9 + hl]
                mcell = self.rowS[0:1, 10 + hl: 11 + hl]
                E("DVE", OP("tensor_tensor", mcell, kcell, qcell, op=ALU.mult),
                  reads=["rowS"], writes=["rowS"])
                E("DVE", OP("tensor_scalar_add", mcell, mcell, 1.0e-30),
                  reads=["rowS"], writes=["rowS"])
                E("ACT", OP("activation", mcell, mcell, AF.Ln),
                  reads=["rowS"], writes=["rowS"])
                E("ACT", OP("activation", mcell, mcell, AF.Exp, scale=0.5),
                  reads=["rowS"], writes=["rowS"])
            for hl in range(2):
                DMA("SP", "slp", self.rowSLP[0:1, :],
                    d["alibik"][L, 2 * p + hl, 3:4], writes=["rowSLP"])
                mcell = self.rowS[0:1, 10 + hl: 11 + hl]
                for w in range(NW):
                    rw = self.rowA if w % 2 == 0 else self.rowB
                    rk = "rowA" if w % 2 == 0 else "rowB"
                    E("DVE", OP("tensor_single_scalar", rw[0:1, :],
                                self.rowSLP[0:1, w * WQ:(w + 1) * WQ], mcell, op=ALU.add),
                      reads=["rowSLP", "rowS"], writes=[rk])
                    E("DVE", OP("tensor_scalar_mul", pC[64:65, QAx[hl] + w * WQ: QAx[hl] + (w + 1) * WQ],
                                rw[0:1, :], -1.0),
                      reads=[rk], writes=[f"qaug{hl}"])
            for tt in range(16):
                bv = 6 + (tt % 2)
                self.mm_group(f"ps{bv}", ps[bv][:, 0:128],
                              [(xn[:, dc * 2048 + tt * 128: dc * 2048 + (tt + 1) * 128],
                                pB[:, base + 2048 + dc * 128: base + 2048 + (dc + 1) * 128])
                               for dc in range(NDC)],
                              reads=[wkq] + [f"xn_{dc}_{tt // 4}" for dc in range(NDC)])
                vt = VP + tt * 132
                E("DVE", OP("tensor_tensor", pC[:, vt:vt + 64], ps[bv][:, 0:64], sc[3][:, 0:64], op=ALU.add),
                  reads=[f"ps{bv}", "sc3"], writes=["vp"])
                E("DVE", OP("tensor_tensor", pC[:, vt + 65:vt + 129], ps[bv][:, 64:128], sc[3][:, 64:128], op=ALU.add),
                  reads=[f"ps{bv}", "sc3"], writes=["vp"])
            for w in range(NW):
                nkt = 4 * (w + 1)
                ng = nkt // 2
                for hl in range(2):
                    hh = p * 2 + hl
                    KA, QA = KAx[hl], QAx[hl]
                    psY = ps[4 + hl]
                    yk = f"ps{4 + hl}"
                    for g in range(ng + 2):
                        pq = self.psq01 if g % 2 == 0 else self.psq23
                        kbase = 2 * (g % 2)
                        if g < ng:
                            for j2 in range(2):
                                kt = 2 * g + j2
                                sl = pq[:, j2 * 512:(j2 + 1) * 512]
                                self.mm_group(f"ps{kbase + j2}", sl,
                                              [(pC[0:68, KA + kt * 128:KA + (kt + 1) * 128],
                                                pC[0:68, QA + w * WQ:QA + (w + 1) * WQ])],
                                              reads=[f"kaug{hl}", f"qaug{hl}"])
                                if kt >= 4 * w:
                                    sub = kt - 4 * w
                                    E("DVE", OP("tensor_tensor",
                                        sl[:, sub * 128:(sub + 1) * 128], sl[:, sub * 128:(sub + 1) * 128],
                                        self.cmaskt[:], op=ALU.add),
                                      reads=["cmaskt", f"ps{kbase + j2}"], writes=[f"ps{kbase + j2}"])
                                    if sub > 0:
                                        E("DVE", OP("tensor_scalar_add",
                                            sl[:, 0:sub * 128], sl[:, 0:sub * 128], NEG),
                                          reads=[f"ps{kbase + j2}"], writes=[f"ps{kbase + j2}"])
                        if g >= 2:
                            g2 = g - 2
                            for j2 in range(2):
                                kt2 = 2 * g2 + j2
                                ptb2 = PTb + (g2 % 2) * 1024 + j2 * 512
                                self.mm_group(yk, psY[0:65, :],
                                              [(pC[:, VP + kt2 * 132 + hl * 65: VP + kt2 * 132 + hl * 65 + 65],
                                                pC[:, ptb2:ptb2 + 512])],
                                              reads=["vp", f"pt{g2 % 2}"], acc_cont=(kt2 > 0),
                                              group_end=(kt2 == nkt - 1))
                        if g < ng:
                            ptb = PTb + (g % 2) * 1024
                            E("ACT", OP("activation", pC[:, ptb:ptb + 1024], pq[:, 0:1024], AF.Exp),
                              reads=[f"ps{kbase}", f"ps{kbase + 1}"], writes=[f"pt{g % 2}"])
                for hl in range(2):
                    hh = p * 2 + hl
                    psY = ps[4 + hl]
                    yk = f"ps{4 + hl}"
                    E("ACT", OP("activation", self.rowM[0:1, :], psY[64:65, :], AF.Ln),
                      reads=[yk], writes=["rowM"])
                    E("ACT", OP("activation", self.rowAb[0:1, :], self.rowM[0:1, :], AF.Exp, scale=-1.0),
                      reads=["rowM"], writes=["rowAb"])
                    self.mm_group("ps6", ps[6][0:64, :], [(self.onesRb[0:1, 0:64], self.rowAb[0:1, :])],
                                  reads=["rowAb", "onesRb"])
                    E("DVE", OP("tensor_copy", sc[2][0:64, :], ps[6][0:64, :]),
                      reads=["ps6"], writes=["sc2"])
                    ybase = (hh // 2) * 2048 + w * WQ
                    yrow = (hh % 2) * 64
                    E("DVE", OP("tensor_tensor",
                        pA[yrow:yrow + 64, ybase:ybase + WQ], psY[0:64, :], sc[2][0:64, :], op=ALU.mult),
                      reads=[yk, "sc2"], writes=[f"yT_{hh // 2}_{w}"])
        if "dbg_yT" in self.dbg_d and self.stop_after == "C":
            DMA("POOL", "dbgyT", self.dbg_d["dbg_yT"][b], pA[:],
                reads=[f"yT_{c}_{w}" for c in range(NDC) for w in range(NW)])
        em.fence(["pBx"])
        em.dmag("POOL", "ow",
                [(pB[:, dc * 1024:(dc + 1) * 1024], owsrc[dc * 128:(dc + 1) * 128, :])
                 for dc in range(NDC)], writes=["ow"])
        for w in range(NW):
            for dt in range(NDC):
                xb_key = "sc0" if dt % 2 == 0 else "sc3"
                xbuf = sc[0] if dt % 2 == 0 else sc[3]
                if L == 0:
                    DMA("SP", f"xrd{dt % 2}", xbuf[:], self.x_d[b, dt * 128:(dt + 1) * 128, w * WQ:(w + 1) * WQ],
                        writes=[xb_key])
                else:
                    DMA("SP", f"x1rd{dt % 2}", xbuf[:], self.x1scr[b, dt * 128:(dt + 1) * 128, w * WQ:(w + 1) * WQ],
                        reads=[f"x1_{dt}_{w}"], writes=[xb_key])
                bo = dt % 2
                self.mm_group(f"ps{bo}", ps[bo][:, :],
                              [(pB[:, dc * 1024 + dt * 128: dc * 1024 + (dt + 1) * 128],
                                pA[:, dc * 2048 + w * WQ: dc * 2048 + (w + 1) * WQ])
                               for dc in range(NDC)],
                              reads=["ow"] + [f"yT_{dc}_{w}" for dc in range(NDC)])
                asl = (slice(None), slice(dt * 2048 + w * WQ, dt * 2048 + (w + 1) * WQ))
                E("ACT", OP("activation", sc[1][:], acc[asl], AF.Silu),
                  reads=[self.acck(dt, w)], writes=["sc1"])
                obcol = self.bias(ob, dt)
                E("DVE", OP("scalar_tensor_tensor", sc[2][:], ps[bo][:, :], obcol, sc[1][:],
                            op0=ALU.add, op1=ALU.mult),
                  reads=[f"ps{bo}", "bfm", "sc1"], writes=["sc2"])
                E("DVE", OP("tensor_tensor", xbuf[:], xbuf[:], sc[2][:], op=ALU.add),
                  reads=[xb_key, "sc2"], writes=[xb_key])
                if L == 0:
                    DMA("SP", f"x1wr{dt % 2}", self.x1scr[b, dt * 128:(dt + 1) * 128, w * WQ:(w + 1) * WQ],
                        xbuf[:], reads=[xb_key], writes=[f"x1_{dt}_{w}"])
                    E("ACT", OP("square", tb[dt % 2][:], xbuf[:]), reads=[xb_key], writes=[f"tb{dt % 2}"])
                    E("DVE", OP("tensor_copy", sc[2][:].bitcast(BF16)[:, 0:512], xbuf[:]),
                      reads=[xb_key, "sc2"], writes=["sc2"])
                    self.mm_group("ps6", ps[6][0:1, :], [(self.onesCb[:], sc[2][:].bitcast(BF16)[:, 0:512])],
                                  reads=["sc2", "onesCb"], acc_cont=(dt > 0),
                                  group_end=(dt == NDC - 1))
                    self.mm_group("ps7", ps[7][0:1, :], [(self.onesCb[:], tb[dt % 2][:])],
                                  reads=[f"tb{dt % 2}", "onesCb"], acc_cont=(dt > 0),
                                  group_end=(dt == NDC - 1))
                else:
                    DMA("SP", f"outwr{dt % 2}", self.out_d[b, dt * 128:(dt + 1) * 128, w * WQ:(w + 1) * WQ],
                        xbuf[:], reads=[xb_key], writes=[f"outw{dt % 2}"])
            if L == 0:
                E("DVE", OP("tensor_scalar_mul", self.rowAb[:], ps[6][0:1, :], 1.0 / D),
                  reads=["ps6"], writes=["rowAb"])
                E("DVE", OP("tensor_scalar_mul", self.rowB[:], ps[7][0:1, :], 1.0 / D),
                  reads=["ps7"], writes=["rowB"])
                E("DVE", OP("tensor_tensor", self.rowM[:], self.rowAb[:], self.rowAb[:], op=ALU.mult),
                  reads=["rowAb"], writes=["rowM"])
                E("DVE", OP("tensor_tensor", self.rowB[:], self.rowB[:], self.rowM[:], op=ALU.subtract),
                  reads=["rowB", "rowM"], writes=["rowB"])
                E("DVE", OP("tensor_scalar_add", self.rowB[:], self.rowB[:], 1.0e-5),
                  reads=["rowB"], writes=["rowB"])
                E("ACT", OP("activation", self.rowM[:], self.rowB[:], AF.Sqrt),
                  reads=["rowB"], writes=["rowM"])
                E("DVE", OP("reciprocal", self.rowBb[:], self.rowM[:]),
                  reads=["rowM"], writes=["rowBb"])
                self.mm_group("ps4", ps[4][:, :], [(self.onesRb[:], self.rowAb[:])],
                              reads=["rowAb", "onesRb"])
                self.mm_group("ps5", ps[5][:, :], [(self.onesRb[:], self.rowBb[:])],
                              reads=["rowBb", "onesRb"])
                for dt in range(NDC):
                    xb_key = "sc0" if dt % 2 == 0 else "sc3"
                    xbuf = sc[0] if dt % 2 == 0 else sc[3]
                    DMA("SP", f"x1rd{dt % 2}", xbuf[:], self.x1scr[b, dt * 128:(dt + 1) * 128, w * WQ:(w + 1) * WQ],
                        reads=[f"x1_{dt}_{w}"], writes=[xb_key])
                    E("DVE", OP("tensor_tensor", sc[2][:], xbuf[:], ps[4][:, :], op=ALU.subtract),
                      reads=[xb_key, "ps4"], writes=["sc2"])
                    E("DVE", OP("tensor_tensor",
                        self.xn[:, dt * 2048 + w * WQ: dt * 2048 + (w + 1) * WQ], sc[2][:], ps[5][:, :], op=ALU.mult),
                      reads=["sc2", "ps5"], writes=[f"xn_{dt}_{w}"])
        if L == 0 and self.stop_after == "C" and "dbg_xn" in self.dbg_d:
            em.dma("POOL", "dbgxn", self.dbg_d["dbg_xn"][b], self.xn[:],
                   reads=[f"xn_{dc}_{w}" for dc in range(NDC) for w in range(NW)])
        if L == 0 and self.stop_after == "C" and "dbg_x1" in self.dbg_d:
            for dt in range(NDC):
                for w in range(NW):
                    DMA("SP", "dbgx1", self.dbg_d["dbg_x1"][b, :, dt * 2048 + w * WQ: dt * 2048 + (w + 1) * WQ],
                        self.x1scr[b, dt * 128:(dt + 1) * 128, w * WQ:(w + 1) * WQ],
                        reads=[f"x1_{dt}_{w}"])

    def phase_D(self, b):
        em, E, DMA = self.em, self.em.emit, self.em.dma
        em.fence(["pAx", "pBx", "pCx"])
        ps, sc, tb, acc, xn, pA, pB, pC = self.ps, self.sc, self.tb, self.acc, self.xn, self.pA, self.pB, self.pC
        d = self.d_in
        first = True
        for e in range(8):
            for jj in range(4):
                for j2 in range(2):
                    j = jj * 2 + j2
                    gk, uk = f"wg{j2}", f"wu{j2}"
                    em.dmag("POOL", gk,
                            [(pA[:, j2 * 4096 + dc * 512: j2 * 4096 + (dc + 1) * 512],
                              d["l1_wg"][e, dc * 128:(dc + 1) * 128, j * 512:(j + 1) * 512])
                             for dc in range(NDC)], writes=[gk])
                    em.dmag("POOL", uk,
                            [(pA[:, 8192 + j2 * 4096 + dc * 512: 8192 + j2 * 4096 + (dc + 1) * 512],
                              d["l1_wu"][e, dc * 128:(dc + 1) * 128, j * 512:(j + 1) * 512])
                             for dc in range(NDC)], writes=[uk])
                    wds = j % 3
                    em.dmag("POOL", f"wd{wds}",
                            [(pC[:, wds * 4096 + fc * 1024: wds * 4096 + (fc + 1) * 1024],
                              d["l1_wd"][e, j * 512 + fc * 128: j * 512 + (fc + 1) * 128, :])
                             for fc in range(4)], writes=[f"wd{wds}"])
                for w in range(NW):
                    for j2 in range(2):
                        gk, uk = f"wg{j2}", f"wu{j2}"
                        for u in range(4):
                            bx = 2 * (u % 2)
                            self.mm_group(f"ps{bx}", ps[bx][:, :],
                                          [(pA[:, j2 * 4096 + dc * 512 + u * 128: j2 * 4096 + dc * 512 + (u + 1) * 128],
                                            xn[:, dc * 2048 + w * WQ: dc * 2048 + (w + 1) * WQ])
                                           for dc in range(NDC)],
                                          reads=[gk] + [f"xn_{dc}_{w}" for dc in range(NDC)])
                            self.mm_group(f"ps{bx + 1}", ps[bx + 1][:, :],
                                          [(pA[:, 8192 + j2 * 4096 + dc * 512 + u * 128: 8192 + j2 * 4096 + dc * 512 + (u + 1) * 128],
                                            xn[:, dc * 2048 + w * WQ: dc * 2048 + (w + 1) * WQ])
                                           for dc in range(NDC)],
                                          reads=[uk] + [f"xn_{dc}_{w}" for dc in range(NDC)])
                            E("ACT", OP("activation", tb[u % 2][:], ps[bx][:, :], AF.Silu),
                              reads=[f"ps{bx}"], writes=[f"tb{u % 2}"])
                            mb = pB
                            mo = (w % 2) * 4096
                            mc = mo + (j2 * 4 + u) * 512
                            E("DVE", OP("tensor_tensor",
                                mb[:, mc:mc + 512], ps[bx + 1][:, :], tb[u % 2][:], op=ALU.mult),
                              reads=[f"ps{bx + 1}", f"tb{u % 2}"], writes=[f"mbuf{w % 2}"])
                    for dt in range(NDC):
                        by = 4 + (dt % 2)
                        mb = pB
                        mo = (w % 2) * 4096
                        s0, s1 = (jj * 2) % 3, (jj * 2 + 1) % 3
                        self.mm_group(f"ps{by}", ps[by][:, :],
                                      [(pC[:, (s0 if fc < 4 else s1) * 4096 + (fc % 4) * 1024 + dt * 128:
                                            (s0 if fc < 4 else s1) * 4096 + (fc % 4) * 1024 + (dt + 1) * 128],
                                        mb[:, mo + fc * 512: mo + (fc + 1) * 512])
                                       for fc in range(8)],
                                      reads=[f"wd{s0}", f"wd{s1}", f"mbuf{w % 2}"])
                        asl = (slice(None), slice(dt * 2048 + w * WQ, dt * 2048 + (w + 1) * WQ))
                        if first and jj == 0:
                            E("DVE", OP("tensor_copy", acc[asl], ps[by][:, :]),
                              reads=[f"ps{by}"], writes=[self.acck(dt, w)])
                        else:
                            E("DVE", OP("tensor_tensor", 
                                acc[asl], acc[asl], ps[by][:, :], op=ALU.add),
                              reads=[f"ps{by}", self.acck(dt, w)], writes=[self.acck(dt, w)])
            first = False
        if self.stop_after == "D" and "dbg_acc" in self.dbg_d:
            DMA("SP", "dbgacc", self.dbg_d["dbg_acc"][b], acc[:],
                reads=[self.acck(dt, w) for dt in range(NDC) for w in range(NW)])


def run(builder, shared, x_core, core_ids=(0,), trace=False):
    im = dict(shared)
    im["x"] = x_core
    in_maps = [im] * len(core_ids)
    return run_bass_kernel_spmd(builder.nc, in_maps, core_ids=list(core_ids), trace=trace)


def kernel(**inputs):
    shared, per_core_x = host_prepare(inputs)
    b = Builder(n_batches=2).build()
    in_maps = []
    for c in range(8):
        im = dict(shared)
        im["x"] = per_core_x[c]
        in_maps.append(im)
    res = run_bass_kernel_spmd(b.nc, in_maps, core_ids=list(range(8)))
    outs = [np.asarray(r["out"]) for r in res.results]
    full = np.concatenate([o.transpose(0, 2, 1) for o in outs], axis=0)
    return np.ascontiguousarray(full.astype(np.float32))


# version 13 (final)
# speedup vs baseline: 1.1037x; 1.1037x over previous
import numpy as np
import ml_dtypes
from contextlib import ExitStack

import concourse.bass as bass
import concourse.mybir as mybir
from concourse.bass_utils import run_bass_kernel_spmd

F32 = mybir.dt.float32
BF16 = mybir.dt.bfloat16
AF = mybir.ActivationFunctionType
ALU = mybir.AluOpType
AX = mybir.AxisListType

NEG = -1.0e9
D = 1024
T = 2048
NDC = 8
NW = 4
WQ = 512
NH = 16
HD = 64

ENGS = ("PE", "DVE", "ACT", "POOL", "SP")


def OP(name, *args, **kwargs):
    def b(h):
        return getattr(h, name)(*args, **kwargs)
    return b

BOFF = {"qb": 0, "kvb": 8, "ob": 24, "fc1b": 32, "gm": 272, "sfc1b": 392,
        "sgm": 424, "sfc2b": 440, "battn": 448, "bproj": 472}
NBIAS = 480


class Emitter:
    def __init__(self, nc, ctx):
        self.nc = nc
        self.ctx = ctx
        self.ops = {e: [] for e in ENGS}
        self.cnt = {e: 0 for e in ENGS}
        self.esem = {e: ctx.enter_context(nc.semaphore(f"es_{e}")) for e in ENGS}
        self.dsem = {}
        self.buf = {}
        self.wm = {}
        self.aliasmap = {}

    def set_alias(self, prefixes, pool):
        for p in prefixes:
            self.aliasmap[p] = pool

    def _expand(self, keys):
        alias = []
        for k in keys:
            for pref, pool in self.aliasmap.items():
                if k.startswith(pref):
                    alias.append(pool)
                    break
        return list(keys), alias

    def _dma_sem(self, key):
        if key not in self.dsem:
            self.dsem[key] = [self.ctx.enter_context(self.nc.semaphore(f"ds_{key}")), 0]
        return self.dsem[key]

    def _need(self, waiter, token, waits):
        if token is None:
            return
        if token[0] == "E":
            _, eng, v = token
            pid = ("E", eng)
            sem = self.esem[eng]
        else:
            _, semkey, v = token
            v = self.dsem[semkey][1]
            pid = ("D", semkey)
            sem = self.dsem[semkey][0]
        if self.wm.get((waiter, pid), 0) >= v:
            return
        self.wm[(waiter, pid)] = v
        waits.append((sem, v))

    def _gather(self, waiter, r_role, w_role, w_alias=()):
        waits = []
        for key in r_role:
            rec = self.buf.get(key)
            if rec is not None:
                self._need(waiter, rec["w"], waits)
        for key in w_role:
            rec = self.buf.get(key)
            if rec is not None:
                self._need(waiter, rec["w"], waits)
                for rid, tok in list(rec["r"].items()):
                    self._need(waiter, tok, waits)
        for key in w_alias:
            rec = self.buf.get(key)
            if rec is not None:
                self._need(waiter, rec["w"], waits)
        return waits

    def _record(self, tok, owner, r_all, w_all):
        for key in w_all:
            self.buf[key] = {"w": tok, "r": {}}
        for key in r_all:
            rec = self.buf.setdefault(key, {"w": None, "r": {}})
            rec["r"][owner] = tok

    def emit(self, eng, builder, reads=(), writes=()):
        rr, ra = self._expand(reads)
        wr, wa = self._expand(writes)
        waits = self._gather(eng, rr, wr, wa)
        self.cnt[eng] += 1
        tok = ("E", eng, self.cnt[eng])
        self._record(tok, eng, rr + ra, wr)
        self.ops[eng].append((waits, builder, self.esem[eng], 1))
        return tok

    def dma(self, queue, semkey, out_ap, in_ap, reads=(), writes=()):
        rr, ra = self._expand(reads)
        wr, wa = self._expand(writes)
        waits = self._gather(queue, rr, wr, wa)
        ds = self._dma_sem(semkey)
        ds[1] += 16
        tok = ("D", semkey, ds[1])
        self._record(tok, queue, rr + ra, wr)

        def b(h, o=out_ap, i=in_ap):
            return h.dma_start(out=o, in_=i)

        self.ops[queue].append((waits, b, ds[0], 16))
        return tok

    def dmag(self, queue, semkey, transfers, reads=(), writes=()):
        rr, ra = self._expand(reads)
        wr, wa = self._expand(writes)
        waits = self._gather(queue, rr, wr, wa)
        ds = self._dma_sem(semkey)
        first = True
        for out_ap, in_ap in transfers:
            ds[1] += 16

            def b(h, o=out_ap, i=in_ap):
                return h.dma_start(out=o, in_=i)

            self.ops[queue].append((waits if first else [], b, ds[0], 16))
            first = False
        tok = ("D", semkey, ds[1])
        self._record(tok, queue, rr + ra, wr)
        return tok

    def fence(self, pools, eng="SP"):
        waits = self._gather(eng, [], list(pools), [])
        self.cnt[eng] += 1
        tok = ("E", eng, self.cnt[eng])
        self._record(tok, eng, [], list(pools))
        self.ops[eng].append((waits, lambda h: h.nop(), self.esem[eng], 1))
        return tok

    def finalize_waits(self):
        pairs = [(ds[0], ds[1]) for ds in self.dsem.values()]

        def b(h):
            for sem, val in pairs:
                h.wait_ge(sem, val)
            return h.nop()

        self.ops["SP"].append(([], b, self.esem["SP"], 1))

    def replay(self, block):
        def run(eng):
            def body(h):
                for waits, builder, sem, inc in self.ops[eng]:
                    for s, v in waits:
                        h.wait_ge(s, v)
                    inst = builder(h)
                    if inst is not None:
                        inst.then_inc(sem, inc)
            return body
        block.sync(run("SP"))
        block.gpsimd(run("POOL"))
        block.tensor(run("PE"))
        block.vector(run("DVE"))
        block.scalar(run("ACT"))


def fm(v):
    return np.ascontiguousarray(np.asarray(v, np.float32).reshape(-1, 128).T)


def pair_perm(F2, half):
    F = F2 // 2
    idx = []
    for j in range(F // half):
        idx.extend(range(j * half, (j + 1) * half))
        idx.extend(range(F + j * half, F + (j + 1) * half))
    return np.array(idx)


def host_prepare(inputs):
    f32 = np.float32
    g = {}
    g["l0_qw"] = np.ascontiguousarray(inputs["L0_attn_q_w"], f32)
    g["l0_kvw"] = np.ascontiguousarray(inputs["L0_attn_kv_w"], f32)
    g["l0_ow"] = np.ascontiguousarray(inputs["L0_attn_o_w"], f32)
    perm_m = pair_perm(2048, 512)
    g["l0_fc1w"] = np.ascontiguousarray(
        np.asarray(inputs["L0_ffn_fc1_w"], f32).transpose(0, 2, 1)[:, :, perm_m])
    g["l0_fc2w"] = np.ascontiguousarray(
        np.asarray(inputs["L0_ffn_fc2_w"], f32).transpose(0, 2, 1))
    perm_s = pair_perm(4096, 512)
    g["l0_sfc1w"] = np.ascontiguousarray(np.asarray(inputs["L0_ffn_s_fc1_w"], f32).T[:, perm_s])
    g["l0_sfc2w"] = np.ascontiguousarray(np.asarray(inputs["L0_ffn_s_fc2_w"], f32).T)
    gw = np.zeros((128, NDC * 15), f32)
    gwt = np.asarray(inputs["L0_ffn_gate_w"], f32)
    for dc in range(NDC):
        gw[:, dc * 15:(dc + 1) * 15] = gwt[:, dc * 128:(dc + 1) * 128].T
    g["l0_gatew"] = gw
    g["l0_fc2b"] = np.ascontiguousarray(inputs["L0_ffn_fc2_b"], f32)
    g["l1_wattn"] = np.ascontiguousarray(inputs["L1_attn_W_attn"], f32)
    g["l1_wproj"] = np.ascontiguousarray(inputs["L1_attn_W_proj"], f32)
    g["l1_temp"] = np.asarray(inputs["L1_attn_temperature"], f32).reshape(1, 1)
    g["l1_wg"] = np.ascontiguousarray(inputs["L1_ffn_Wg"], f32)
    g["l1_wu"] = np.ascontiguousarray(inputs["L1_ffn_Wu"], f32)
    g["l1_wd"] = np.ascontiguousarray(inputs["L1_ffn_Wd"], f32)
    bfm = np.zeros((128, NBIAS), f32)
    bfm[:, 0:8] = fm(inputs["L0_attn_q_b"])
    bfm[:, 8:24] = fm(inputs["L0_attn_kv_b"])
    bfm[:, 24:32] = fm(inputs["L0_attn_o_b"])
    fc1b_p = np.asarray(inputs["L0_ffn_fc1_b"], f32)[:, perm_m]
    for e in range(15):
        bfm[:, 32 + e * 16:48 + e * 16] = fm(fc1b_p[e])
        bfm[:, 272 + e * 8:280 + e * 8] = fm(np.asarray(inputs["L0_ffn_geglu_mult"], f32)[e])
    bfm[:, 392:424] = fm(np.asarray(inputs["L0_ffn_s_fc1_b"], f32)[perm_s])
    bfm[:, 424:440] = fm(inputs["L0_ffn_s_geglu_mult"])
    bfm[:, 440:448] = fm(inputs["L0_ffn_s_fc2_b"])
    bfm[:, 448:472] = fm(inputs["L1_attn_b_attn"])
    bfm[:, 472:480] = fm(inputs["L1_attn_b_proj"])
    g["b_fm"] = bfm
    g["ident"] = np.eye(128, dtype=f32)
    i = np.arange(128)
    g["cmask"] = np.where(i[:, None] <= i[None, :], 0.0, NEG).astype(f32)
    ak = np.zeros((2, 16, 4, T), ml_dtypes.bfloat16)
    kidx = np.arange(T, dtype=np.float64)
    for h in range(16):
        for li, s in enumerate(((h + 1) / 16.0, 2.0 ** (-(h + 1) / 2.0))):
            v = s * kidx
            hi = v.astype(ml_dtypes.bfloat16)
            r1 = v - hi.astype(np.float64)
            mid = r1.astype(ml_dtypes.bfloat16)
            lo = (r1 - mid.astype(np.float64)).astype(ml_dtypes.bfloat16)
            ak[li, h, 0], ak[li, h, 1], ak[li, h, 2] = hi, mid, lo
            ak[li, h, 3] = v.astype(ml_dtypes.bfloat16)
    g["alibik"] = ak

    xs = np.asarray(inputs["L0_attn_x"], f32)
    per_core_x = [np.ascontiguousarray(xs[2 * c:2 * c + 2].transpose(0, 2, 1))
                  for c in range(8)]
    return g, per_core_x


class Builder:
    def __init__(self, n_batches=2, stop_after=None, dbg=()):
        self.n_batches = n_batches
        self.stop_after = stop_after
        self.dbg = set(dbg)
        self.nc = bass.Bass("TRN2", target_bir_lowering=False, debug=False)

    def mm_group(self, bank, psum_ap, pairs, reads, extra_writes=(), acc_cont=False,
                 group_end=True):
        def b(h, pairs=tuple(pairs), psum_ap=psum_ap):
            inst = None
            n = len(pairs)
            for i, (lhsT, rhs) in enumerate(pairs):
                inst = h.matmul(psum_ap, lhsT, rhs,
                                start=(i == 0 and not acc_cont),
                                stop=(i == n - 1 and group_end))
            return inst
        return self.em.emit("PE", b, reads=reads, writes=[bank, *extra_writes])

    def bias(self, name, idx):
        return self.bfm[:, BOFF[name] + idx:BOFF[name] + idx + 1]

    def build(self):
        nc = self.nc
        self.d_in = {}

        def din(name, shape, dt=F32):
            self.d_in[name] = nc.dram_tensor(name, list(shape), dt, kind="ExternalInput").ap()
            return self.d_in[name]

        self.x_d = din("x", (2, 1024, T))
        din("l0_qw", (1024, 1024)); din("l0_kvw", (1024, 2048)); din("l0_ow", (1024, 1024))
        din("l0_fc1w", (15, 1024, 2048)); din("l0_fc2w", (15, 1024, 1024))
        din("l0_sfc1w", (1024, 4096)); din("l0_sfc2w", (2048, 1024))
        din("l0_gatew", (128, NDC * 15)); din("l0_fc2b", (15, 1024))
        din("l1_wattn", (1024, 3072)); din("l1_wproj", (1024, 1024)); din("l1_temp", (1, 1))
        din("l1_wg", (8, 1024, 4096)); din("l1_wu", (8, 1024, 4096)); din("l1_wd", (8, 4096, 1024))
        din("b_fm", (128, NBIAS)); din("ident", (128, 128)); din("cmask", (128, 128))
        din("alibik", (2, 16, 4, T), BF16)
        self.out_d = nc.dram_tensor("out", [2, 1024, T], F32, kind="ExternalOutput").ap()
        self.x1scr = nc.dram_tensor("x1scr", [2, 1024, T], F32, kind="Internal").ap()
        self.dbg_d = {}
        if "xn" in self.dbg:
            self.dbg_d["dbg_xn"] = nc.dram_tensor("dbg_xn", [self.n_batches, 128, 16384], F32, kind="ExternalOutput").ap()
        if "comb" in self.dbg:
            self.dbg_d["dbg_comb"] = nc.dram_tensor("dbg_comb", [self.n_batches, 128, 240], F32, kind="ExternalOutput").ap()
        if "acc" in self.dbg:
            self.dbg_d["dbg_acc"] = nc.dram_tensor("dbg_acc", [self.n_batches, 128, 16384], F32, kind="ExternalOutput").ap()
        if "yT" in self.dbg:
            self.dbg_d["dbg_yT"] = nc.dram_tensor("dbg_yT", [self.n_batches, 128, 16384], F32, kind="ExternalOutput").ap()
        if "x1" in self.dbg:
            self.dbg_d["dbg_x1"] = nc.dram_tensor("dbg_x1", [self.n_batches, 128, 16384], F32, kind="ExternalOutput").ap()

        ctx = ExitStack()
        with ctx:
            sbt = lambda n, s, dt: ctx.enter_context(nc.sbuf_tensor(n, list(s), dt))
            self.xn = sbt("xn", (128, 16384), BF16)
            self.acc = sbt("acc", (128, 16384), F32)
            self.pA = sbt("pA", (128, 16384), BF16)
            self.pB = sbt("pB", (128, 8192), BF16)
            self.pC = sbt("pC", (128, 12352), BF16)
            self.sc = [sbt(f"sc{i}", (128, 512), F32) for i in range(4)]
            self.tb = [sbt(f"tb{i}", (128, 512), BF16) for i in range(2)]
            self.rowA = sbt("rowA", (1, 512), F32)
            self.rowB = sbt("rowB", (1, 512), F32)
            self.rowM = sbt("rowM", (1, 512), F32)
            self.rowS = sbt("rowS", (1, 16), F32)
            self.rowSLP = sbt("rowSLP", (1, T), BF16)
            self.invcol = sbt("invcol", (128, 1), F32)
            self.comb = sbt("comb", (128, 16 * 15), F32)
            self.combT = sbt("combT", (15, T), F32)
            self.gatew = sbt("gatew", (128, NDC * 15), F32)
            self.bfm = sbt("bfm", (128, NBIAS), F32)
            self.fc2b = sbt("fc2b", (15, 1024), F32)
            self.identt = sbt("identt", (128, 128), F32)
            self.cmaskt = sbt("cmaskt", (128, 128), F32)
            self.onesC = sbt("onesC", (128, 1), F32)
            self.onesCb = sbt("onesCb", (128, 1), BF16)
            self.onesRb = sbt("onesRb", (1, 128), BF16)
            self.rowAb = sbt("rowAb", (1, 512), BF16)
            self.rowBb = sbt("rowBb", (1, 512), BF16)
            self.onesR = sbt("onesR", (1, 128), F32)
            self.ps = [ctx.enter_context(nc.psum_tensor(f"ps{i}", [128, 512], F32))
                       for i in range(8)]
            self.em = Emitter(nc, ctx)
            self.em.set_alias(("wA", "yT_", "wg", "wu"), "pAx")
            self.em.set_alias(("fc2s", "abufM", "pw", "ow", "mbuf"), "pBx")
            self.em.set_alias(("fc2m", "vp", "kaug", "qaug", "pt", "wd"), "pCx")
            self.program()
            self.em.finalize_waits()
            with nc.allow_low_precision(reason="bf16 broadcast-source rows/stats; deliberate, within tolerance"):
                with nc.Block() as block:
                    self.em.replay(block)
        return self

    def program(self):
        em, d = self.em, self.d_in
        em.dmag("SP", "const", [
            (self.gatew[:], d["l0_gatew"][:]),
            (self.bfm[:], d["b_fm"][:]),
            (self.fc2b[:], d["l0_fc2b"][:]),
            (self.identt[:], d["ident"][:]),
            (self.cmaskt[:], d["cmask"][:]),
            (self.rowS[0:1, 4:5], d["l1_temp"][:]),
        ], writes=["gatew", "bfm", "fc2b", "identt", "cmaskt", "rowS"])
        em.emit("DVE", OP("memset", self.onesC[:], 1.0), writes=["onesC"])
        em.emit("DVE", OP("memset", self.onesCb[:], 1.0), writes=["onesCb"])
        em.emit("DVE", OP("memset", self.onesRb[:], 1.0), writes=["onesRb"])
        em.emit("DVE", OP("memset", self.onesR[:], 1.0), writes=["onesR"])
        for b in range(self.n_batches):
            self.phase_A(b)
            if self.stop_after == "A":
                continue
            self.phase_B1(b)
            if self.stop_after == "B1":
                continue
            self.phase_B2(b)
            if self.stop_after == "B2":
                continue
            self.phase_attn(b, 0)
            if self.stop_after == "C":
                continue
            self.phase_D(b)
            if self.stop_after == "D":
                continue
            self.phase_attn(b, 1)
        if "comb" in self.dbg:
            for b in range(self.n_batches):
                pass

    def acck(self, dt, w):
        return f"acc_{dt}_{w}"

    def phase_A(self, b):
        em, E, DMA = self.em, self.em.emit, self.em.dma
        ps, sc, acc, xn, tb = self.ps, self.sc, self.acc, self.xn, self.tb
        em.dmag("SP", "xload",
                [(acc[:, dc * 2048:(dc + 1) * 2048], self.x_d[b, dc * 128:(dc + 1) * 128, :])
                 for dc in range(NDC)],
                writes=[self.acck(dc, w) for dc in range(NDC) for w in range(NW)])
        for w in range(NW):
            wsl = slice(w * WQ, (w + 1) * WQ)
            for dc in range(NDC):
                ac = acc[:, dc * 2048 + w * WQ: dc * 2048 + (w + 1) * WQ]
                tcp = tb[dc % 2]
                self.mm_group("ps6", ps[6][0:1, :], [(self.onesC[:], ac)],
                              reads=[self.acck(dc, w), "onesC"], acc_cont=(dc > 0),
                              group_end=(dc == NDC - 1))
                E("ACT", OP("square", tcp[:], ac), reads=[self.acck(dc, w)], writes=[f"tb{dc % 2}"])
                self.mm_group("ps7", ps[7][0:1, :], [(self.onesCb[:], tcp[:])],
                              reads=[f"tb{dc % 2}", "onesCb"], acc_cont=(dc > 0),
                              group_end=(dc == NDC - 1))
            E("DVE", OP("tensor_scalar_mul", self.rowAb[:], ps[6][0:1, :], 1.0 / D),
              reads=["ps6"], writes=["rowAb"])
            E("DVE", OP("tensor_scalar_mul", self.rowB[:], ps[7][0:1, :], 1.0 / D),
              reads=["ps7"], writes=["rowB"])
            E("DVE", OP("tensor_tensor", self.rowM[:], self.rowAb[:], self.rowAb[:], op=ALU.mult),
              reads=["rowAb"], writes=["rowM"])
            E("DVE", OP("tensor_tensor", self.rowB[:], self.rowB[:], self.rowM[:], op=ALU.subtract),
              reads=["rowB", "rowM"], writes=["rowB"])
            E("DVE", OP("tensor_scalar_add", self.rowB[:], self.rowB[:], 1.0e-5),
              reads=["rowB"], writes=["rowB"])
            E("ACT", OP("activation", self.rowM[:], self.rowB[:], AF.Sqrt),
              reads=["rowB"], writes=["rowM"])
            E("DVE", OP("reciprocal", self.rowBb[:], self.rowM[:]),
              reads=["rowM"], writes=["rowBb"])
            self.mm_group("ps4", ps[4][:, :], [(self.onesRb[:], self.rowAb[:])],
                          reads=["rowAb", "onesRb"], extra_writes=[])
            self.mm_group("ps5", ps[5][:, :], [(self.onesRb[:], self.rowBb[:])],
                          reads=["rowBb", "onesRb"])
            for dc in range(NDC):
                ac = acc[:, dc * 2048 + w * WQ: dc * 2048 + (w + 1) * WQ]
                sx = sc[dc % 2]
                E("DVE", OP("tensor_tensor", sc[2][:], ac, ps[4][:, :], op=ALU.subtract),
                  reads=[self.acck(dc, w), "ps4"], writes=["sc2"])
                E("DVE", OP("tensor_tensor", sx[:], sc[2][:], ps[5][:, :], op=ALU.mult),
                  reads=["sc2", "ps5"], writes=[f"sc{dc % 2}"])
                E("DVE", OP("tensor_copy", 
                    self.xn[:, dc * 2048 + w * WQ: dc * 2048 + (w + 1) * WQ], sx[:]),
                  reads=[f"sc{dc % 2}"], writes=[f"xn_{dc}_{w}"])
                for tt in range(4):
                    E("PE", OP("matmul", 
                        ps[tt][0:128, 0:15], sx[:, tt * 128:(tt + 1) * 128],
                        self.gatew[:, dc * 15:(dc + 1) * 15],
                        start=(dc == 0), stop=(dc == NDC - 1)),
                      reads=[f"sc{dc % 2}", "gatew"], writes=[f"ps{tt}"])
            for tt in range(4):
                g = ps[tt][0:128, 0:15]
                s3 = sc[3]
                E("DVE", OP("tensor_reduce", s3[:, 0:1], g, axis=AX.X, op=ALU.max, negate=True),
                  reads=[f"ps{tt}"], writes=["sc3"])
                E("ACT", OP("activation", s3[:, 1:16], g, AF.Exp, bias=s3[:, 0:1]),
                  reads=[f"ps{tt}", "sc3"], writes=["sc3"])
                E("DVE", OP("tensor_reduce", s3[:, 16:17], s3[:, 1:16], axis=AX.X, op=ALU.add),
                  reads=["sc3"], writes=["sc3"])
                E("DVE", OP("reciprocal", s3[:, 17:18], s3[:, 16:17]),
                  reads=["sc3"], writes=["sc3"])
                E("DVE", OP("tensor_scalar", s3[:, 32:47], s3[:, 1:16], s3[:, 17:18], None, op0=ALU.mult),
                  reads=["sc3"], writes=["sc3"])
                E("DVE", OP("max", out=s3[:, 48:56], in_=s3[:, 32:47]),
                  reads=["sc3"], writes=["sc3"])
                E("DVE", OP("tensor_reduce", s3[:, 56:57], s3[:, 48:51], axis=AX.X, op=ALU.add),
                  reads=["sc3"], writes=["sc3"])
                E("DVE", OP("tensor_scalar_add", s3[:, 56:57], s3[:, 56:57], 1.0e-20),
                  reads=["sc3"], writes=["sc3"])
                E("DVE", OP("reciprocal", s3[:, 57:58], s3[:, 56:57]),
                  reads=["sc3"], writes=["sc3"])
                E("DVE", OP("tensor_scalar", s3[:, 64:79], s3[:, 32:47], s3[:, 50:51], None, op0=ALU.is_ge),
                  reads=["sc3"], writes=["sc3"])
                E("DVE", OP("tensor_scalar", s3[:, 80:95], s3[:, 32:47], s3[:, 57:58], None, op0=ALU.mult),
                  reads=["sc3"], writes=["sc3"])
                ci = (w * 4 + tt) * 15
                E("DVE", OP("tensor_tensor", 
                    self.comb[:, ci:ci + 15], s3[:, 64:79], s3[:, 80:95], op=ALU.mult),
                  reads=["sc3"], writes=["comb"])
                E("PE", OP("transpose", 
                    ps[5][0:15, 0:128], self.comb[:, ci:ci + 15], self.identt[:]),
                  reads=["comb", "identt"], writes=["ps5"])
                cj = (w * 4 + tt) * 128
                E("DVE", OP("tensor_copy", self.combT[0:15, cj:cj + 128], ps[5][0:15, 0:128]),
                  reads=["ps5"], writes=["combT"])
        if "dbg_xn" in self.dbg_d and self.stop_after == "A":
            DMA("POOL", "dbgxn", self.dbg_d["dbg_xn"][b], self.xn[:],
                reads=[f"xn_{dc}_{w}" for dc in range(NDC) for w in range(NW)])
        if "dbg_comb" in self.dbg_d:
            DMA("SP", "dbgcomb", self.dbg_d["dbg_comb"][b], self.comb[:], reads=["comb"])

    def phase_B1(self, b):
        em, E, DMA = self.em, self.em.emit, self.em.dma
        em.fence(["pAx", "pBx", "pCx"])
        ps, sc, tb, acc, xn, pA, pB, pC = self.ps, self.sc, self.tb, self.acc, self.xn, self.pA, self.pB, self.pC
        d = self.d_in
        for j in range(4):
            wk = f"wA{j % 2}"
            wbase = (j % 2) * 8192
            em.dmag("POOL", wk,
                    [(pA[:, wbase + dc * 1024: wbase + (dc + 1) * 1024],
                      d["l0_sfc1w"][dc * 128:(dc + 1) * 128, j * 1024:(j + 1) * 1024])
                     for dc in range(NDC)], writes=[wk])
            fk = f"fc2s{j % 2}"
            fbase = (j % 2) * 4096
            em.dmag("POOL", fk,
                    [(pB[:, fbase + fc * 1024: fbase + (fc + 1) * 1024],
                      d["l0_sfc2w"][j * 512 + fc * 128: j * 512 + (fc + 1) * 128, :])
                     for fc in range(4)], writes=[fk])
            for w in range(NW):
                wsl = slice(w * WQ, (w + 1) * WQ)
                abase = 8192 + (w % 2) * 2048
                ak = f"abufS{w % 2}"
                for u in range(4):
                    bx = 2 * (u % 2)
                    self.mm_group(f"ps{bx}", ps[bx][:, :],
                                  [(pA[:, wbase + dc * 1024 + u * 128: wbase + dc * 1024 + (u + 1) * 128],
                                    xn[:, dc * 2048 + w * WQ: dc * 2048 + (w + 1) * WQ])
                                   for dc in range(NDC)],
                                  reads=[wk] + [f"xn_{dc}_{w}" for dc in range(NDC)])
                    self.mm_group(f"ps{bx + 1}", ps[bx + 1][:, :],
                                  [(pA[:, wbase + dc * 1024 + 512 + u * 128: wbase + dc * 1024 + 512 + (u + 1) * 128],
                                    xn[:, dc * 2048 + w * WQ: dc * 2048 + (w + 1) * WQ])
                                   for dc in range(NDC)],
                                  reads=[wk] + [f"xn_{dc}_{w}" for dc in range(NDC)])
                    bias_g = self.bias("sfc1b", j * 8 + 4 + u)
                    bias_x = self.bias("sfc1b", j * 8 + u)
                    gmcol = self.bias("sgm", j * 4 + u)
                    E("ACT", OP("activation", 
                        tb[u % 2][:], ps[bx + 1][:, :], AF.Gelu, bias=bias_g),
                      reads=[f"ps{bx + 1}", "bfm"], writes=[f"tb{u % 2}"])
                    E("DVE", OP("tensor_scalar", 
                        sc[2][:], ps[bx][:, :], bias_x, gmcol, op0=ALU.add, op1=ALU.mult),
                      reads=[f"ps{bx}", "bfm"], writes=["sc2"])
                    E("DVE", OP("tensor_tensor", 
                        pC[:, abase + u * 512: abase + (u + 1) * 512], sc[2][:], tb[u % 2][:], op=ALU.mult),
                      reads=["sc2", f"tb{u % 2}"], writes=[ak])
                for dt in range(NDC):
                    by = 4 + (dt % 2)
                    self.mm_group(f"ps{by}", ps[by][:, :],
                                  [(pB[:, fbase + fc * 1024 + dt * 128: fbase + fc * 1024 + (dt + 1) * 128],
                                    pC[:, abase + fc * 512: abase + (fc + 1) * 512])
                                   for fc in range(4)],
                                  reads=[fk, ak])
                    asl = (slice(None), slice(dt * 2048 + w * WQ, dt * 2048 + (w + 1) * WQ))
                    if j == 0:
                        bcol = self.bias("sfc2b", dt)
                        E("DVE", OP("tensor_scalar", 
                            acc[asl], ps[by][:, :], bcol, None, op0=ALU.add),
                          reads=[f"ps{by}", "bfm"], writes=[self.acck(dt, w)])
                    else:
                        E("DVE", OP("tensor_tensor", 
                            acc[asl], acc[asl], ps[by][:, :], op=ALU.add),
                          reads=[f"ps{by}", self.acck(dt, w)], writes=[self.acck(dt, w)])
        if self.stop_after == "B1" and "dbg_acc" in self.dbg_d:
            DMA("SP", "dbgacc", self.dbg_d["dbg_acc"][b], acc[:],
                reads=[self.acck(dt, w) for dt in range(NDC) for w in range(NW)])

    def phase_B2(self, b):
        em, E, DMA = self.em, self.em.emit, self.em.dma
        em.fence(["pBx", "pCx"])
        ps, sc, tb, acc, xn, pA, pB, pC = self.ps, self.sc, self.tb, self.acc, self.xn, self.pA, self.pB, self.pC
        d = self.d_in
        for e in range(15):
            for half in range(2):
                slot = (e * 2 + half) % 3
                em.dmag("POOL", f"fc2m{slot}",
                        [(pC[:, slot * 4096 + fc * 1024: slot * 4096 + (fc + 1) * 1024],
                          d["l0_fc2w"][e, (half * 4 + fc) * 128:(half * 4 + fc + 1) * 128, :])
                         for fc in range(4)], writes=[f"fc2m{slot}"])
            for j in range(2):
                wk = f"wA{j}"
                wbase = j * 8192
                em.dmag("POOL", wk,
                        [(pA[:, wbase + dc * 1024: wbase + (dc + 1) * 1024],
                          d["l0_fc1w"][e, dc * 128:(dc + 1) * 128, j * 1024:(j + 1) * 1024])
                         for dc in range(NDC)], writes=[wk])
            for j in range(2):
                wk = f"wA{j}"
                wbase = j * 8192
                for w in range(NW):
                    cb_row = self.rowAb if w % 2 == 0 else self.rowBb
                    cb_rk = "rowAb" if w % 2 == 0 else "rowBb"
                    cbank = 6 + (w % 2)
                    em.dma("POOL", f"rowcb{w % 2}", cb_row[0:1, :], self.combT[e:e + 1, w * WQ:(w + 1) * WQ],
                           reads=["combT"], writes=[cb_rk])
                    self.mm_group(f"ps{cbank}", ps[cbank][:, :],
                                  [(self.onesRb[:], cb_row[0:1, :])],
                                  reads=[cb_rk, "onesRb"])
                    for u in range(4):
                        fcg = j * 4 + u
                        bx = 2 * (u % 2)
                        self.mm_group(f"ps{bx}", ps[bx][:, :],
                                      [(pA[:, wbase + dc * 1024 + u * 128: wbase + dc * 1024 + (u + 1) * 128],
                                        xn[:, dc * 2048 + w * WQ: dc * 2048 + (w + 1) * WQ])
                                       for dc in range(NDC)],
                                      reads=[wk] + [f"xn_{dc}_{w}" for dc in range(NDC)])
                        self.mm_group(f"ps{bx + 1}", ps[bx + 1][:, :],
                                      [(pA[:, wbase + dc * 1024 + 512 + u * 128: wbase + dc * 1024 + 512 + (u + 1) * 128],
                                        xn[:, dc * 2048 + w * WQ: dc * 2048 + (w + 1) * WQ])
                                       for dc in range(NDC)],
                                      reads=[wk] + [f"xn_{dc}_{w}" for dc in range(NDC)])
                        bias_g = self.bias("fc1b", e * 16 + j * 8 + 4 + u)
                        bias_x = self.bias("fc1b", e * 16 + j * 8 + u)
                        gmcol = self.bias("gm", e * 8 + fcg)
                        E("ACT", OP("activation", 
                            tb[u % 2][:], ps[bx + 1][:, :], AF.Gelu, bias=bias_g),
                          reads=[f"ps{bx + 1}", "bfm"], writes=[f"tb{u % 2}"])
                        sx = sc[u % 2]
                        E("DVE", OP("tensor_scalar", 
                            sx[:], ps[bx][:, :], bias_x, gmcol, op0=ALU.add, op1=ALU.mult),
                          reads=[f"ps{bx}", "bfm"], writes=[f"sc{u % 2}"])
                        E("DVE", OP("tensor_tensor", sx[:], sx[:], tb[u % 2][:], op=ALU.mult),
                          reads=[f"sc{u % 2}", f"tb{u % 2}"], writes=[f"sc{u % 2}"])
                        ac = (u * 4 + w) * 512
                        E("DVE", OP("tensor_tensor",
                            pB[:, ac:ac + 512], sx[:], ps[6 + (w % 2)][:, :], op=ALU.mult),
                          reads=[f"sc{u % 2}", f"ps{6 + (w % 2)}"], writes=[f"abufM_{w}"])
                slot = (e * 2 + j) % 3
                for w in range(NW):
                    for dt in range(NDC):
                        by = 4 + (dt % 2)
                        self.mm_group(f"ps{by}", ps[by][:, :],
                                      [(pC[:, slot * 4096 + fc * 1024 + dt * 128: slot * 4096 + fc * 1024 + (dt + 1) * 128],
                                        pB[:, (fc * 4 + w) * 512: (fc * 4 + w) * 512 + 512])
                                       for fc in range(4)],
                                      reads=[f"fc2m{slot}", f"abufM_{w}"])
                        asl = (slice(None), slice(dt * 2048 + w * WQ, dt * 2048 + (w + 1) * WQ))
                        E("DVE", OP("tensor_tensor", 
                            acc[asl], acc[asl], ps[by][:, :], op=ALU.add),
                          reads=[f"ps{by}", self.acck(dt, w)], writes=[self.acck(dt, w)])
        for w in range(NW):
            for dt in range(NDC):
                by = 4 + (dt % 2)
                self.mm_group(f"ps{by}", ps[by][:, :],
                              [(self.fc2b[:, dt * 128:(dt + 1) * 128],
                                self.combT[:, w * WQ:(w + 1) * WQ])],
                              reads=["fc2b", "combT"])
                asl = (slice(None), slice(dt * 2048 + w * WQ, dt * 2048 + (w + 1) * WQ))
                E("DVE", OP("tensor_tensor", 
                    acc[asl], acc[asl], ps[by][:, :], op=ALU.add),
                  reads=[f"ps{by}", self.acck(dt, w)], writes=[self.acck(dt, w)])
        if self.stop_after == "B2" and "dbg_acc" in self.dbg_d:
            DMA("SP", "dbgacc", self.dbg_d["dbg_acc"][b], acc[:],
                reads=[self.acck(dt, w) for dt in range(NDC) for w in range(NW)])

    def phase_attn(self, b, L):
        em, E, DMA = self.em, self.em.emit, self.em.dma
        em.fence(["pAx", "pBx", "pCx"])
        ps, sc, tb, acc, xn, pA, pB, pC = self.ps, self.sc, self.tb, self.acc, self.xn, self.pA, self.pB, self.pC
        d = self.d_in
        VP = 0
        KAx = (2112, 4160)
        QAx = (6208, 8256)
        PTb = 10304
        SB = (0, 1, 4, 5)
        DEPTH = 3
        if L == 0:
            wsrc, koff, voff = d["l0_kvw"], 0, 1024
            qwsrc = d["l0_qw"]
            qb, kb, vb = ("qb", 0), ("kvb", 0), ("kvb", 8)
            owsrc, ob = d["l0_ow"], "ob"
        else:
            wsrc, koff, voff = d["l1_wattn"], 1024, 2048
            qwsrc = d["l1_wattn"]
            qb, kb, vb = ("battn", 0), ("battn", 8), ("battn", 16)
            owsrc, ob = d["l1_wproj"], "bproj"
        if L == 0:
            E("DVE", OP("memset", self.invcol[:], 0.125), writes=["invcol"])
        else:
            E("DVE", OP("tensor_scalar_mul", self.rowS[0:1, 5:6], self.rowS[0:1, 4:5], 8.0),
              reads=["rowS"], writes=["rowS"])
            E("DVE", OP("reciprocal", self.rowS[0:1, 6:7], self.rowS[0:1, 5:6]),
              reads=["rowS"], writes=["rowS"])
            self.mm_group("ps4", ps[4][0:128, 0:1], [(self.onesR[:], self.rowS[0:1, 6:7])],
                          reads=["rowS", "onesR"])
            E("DVE", OP("tensor_copy", self.invcol[:], ps[4][:, 0:1]),
              reads=["ps4"], writes=["invcol"])

        for hl in range(2):
            E("DVE", OP("memset", pC[64:65, KAx[hl]:KAx[hl] + 2048], 1.0),
              writes=[f"kaug{hl}"])
            E("DVE", OP("memset", pC[64:68, QAx[hl]:QAx[hl] + 2048], 1.0),
              writes=[f"qaug{hl}"])
        E("DVE", OP("memset", pC[:, VP:VP + 16 * 132], 1.0), writes=["vp"])
        for p in range(8):
            s = p % 2
            base = s * 3072
            wkq = f"pw{s}"
            xfers = []
            for dc in range(NDC):
                xfers.append((pB[:, base + dc * 128: base + (dc + 1) * 128],
                              qwsrc[dc * 128:(dc + 1) * 128, p * 128:(p + 1) * 128]))
            for dc in range(NDC):
                xfers.append((pB[:, base + 1024 + dc * 128: base + 1024 + (dc + 1) * 128],
                              wsrc[dc * 128:(dc + 1) * 128, koff + p * 128: koff + (p + 1) * 128]))
            for dc in range(NDC):
                xfers.append((pB[:, base + 2048 + dc * 128: base + 2048 + (dc + 1) * 128],
                              wsrc[dc * 128:(dc + 1) * 128, voff + p * 128: voff + (p + 1) * 128]))
            em.dmag("POOL", wkq, xfers, writes=[wkq])
            for hl in range(2):
                DMA("SP", f"alibik{hl}", pC[65:68, KAx[hl]:KAx[hl] + 2048],
                    d["alibik"][L, 2 * p + hl, 0:3], writes=[f"kaug{hl}"])

            vbcol = self.bias(vb[0], vb[1] + p)
            E("PE", OP("transpose", ps[4][0:1, 0:128], vbcol, self.identt[:]),
              reads=["bfm", "identt"], writes=["ps4"])
            E("DVE", OP("tensor_copy", self.rowAb[0:1, 0:128], ps[4][0:1, 0:128]),
              reads=["ps4"], writes=["rowAb"])
            self.mm_group("ps4", ps[4][0:128, 0:128], [(self.onesRb[:], self.rowAb[0:1, 0:128])],
                          reads=["rowAb", "onesRb"])
            E("DVE", OP("tensor_copy", sc[3][:, 0:128], ps[4][:, 0:128]),
              reads=["ps4"], writes=["sc3"])
            qbc, kbc = self.bias(qb[0], qb[1] + p), self.bias(kb[0], kb[1] + p)
            for w in range(NW):
                bq = w % 2
                wsl = slice(w * WQ, (w + 1) * WQ)
                self.mm_group(f"ps{bq}", ps[bq][:, :],
                              [(pB[:, base + dc * 128: base + (dc + 1) * 128],
                                xn[:, dc * 2048 + w * WQ: dc * 2048 + (w + 1) * WQ])
                               for dc in range(NDC)],
                              reads=[wkq] + [f"xn_{dc}_{w}" for dc in range(NDC)])
                for hl in range(2):
                    r0 = hl * 64
                    E("DVE", OP("tensor_scalar", pC[0:64, QAx[hl] + w * WQ: QAx[hl] + (w + 1) * WQ],
                                ps[bq][r0:r0 + 64, :], qbc[r0:r0 + 64, :], self.invcol[r0:r0 + 64, 0:1],
                                op0=ALU.add, op1=ALU.mult),
                      reads=[f"ps{bq}", "bfm", "invcol"], writes=[f"qaug{hl}"])
                self.mm_group(f"ps{2 + bq}", ps[2 + bq][:, :],
                              [(pB[:, base + 1024 + dc * 128: base + 1024 + (dc + 1) * 128],
                                xn[:, dc * 2048 + w * WQ: dc * 2048 + (w + 1) * WQ])
                               for dc in range(NDC)],
                              reads=[wkq] + [f"xn_{dc}_{w}" for dc in range(NDC)])
                for hl in range(2):
                    r0 = hl * 64
                    E("DVE", OP("tensor_scalar", pC[0:64, KAx[hl] + w * WQ: KAx[hl] + (w + 1) * WQ],
                                ps[2 + bq][r0:r0 + 64, :], kbc[r0:r0 + 64, :], None, op0=ALU.add),
                      reads=[f"ps{2 + bq}", "bfm"], writes=[f"kaug{hl}"])
            for hl in range(2):
                E("DVE", OP("memset", self.rowS[0:1, 0 + 2 * hl: 1 + 2 * hl], 0.0), writes=["rowS"])
                E("DVE", OP("memset", self.rowS[0:1, 8 + hl: 9 + hl], 0.0), writes=["rowS"])
            for w in range(NW):
                for which in ("k", "q"):
                    for hl in range(2):
                        cell = (self.rowS[0:1, 0 + 2 * hl: 1 + 2 * hl] if which == "k"
                                else self.rowS[0:1, 8 + hl: 9 + hl])
                        buf = KAx[hl] if which == "k" else QAx[hl]
                        tbq = self.tb[hl]
                        E("ACT", OP("square", tbq[:64, :], pC[0:64, buf + w * WQ: buf + (w + 1) * WQ]),
                          reads=[f"kaug{hl}" if which == "k" else f"qaug{hl}"], writes=[f"tb{hl}"])
                        self.mm_group(f"ps{6 + hl}", ps[6 + hl][0:1, :], [(self.onesCb[0:64, :], tbq[:64, :])],
                                      reads=[f"tb{hl}", "onesCb"])
                        rw = self.rowA if hl == 0 else self.rowB
                        rk = "rowA" if hl == 0 else "rowB"
                        E("DVE", OP("tensor_reduce", rw[0:1, 0:1], ps[6 + hl][0:1, :], axis=AX.X, op=ALU.max),
                          reads=[f"ps{6 + hl}"], writes=[rk])
                        E("DVE", OP("tensor_tensor", cell, cell, rw[0:1, 0:1], op=ALU.max),
                          reads=[rk, "rowS"], writes=["rowS"])
            for hl in range(2):
                kcell = self.rowS[0:1, 0 + 2 * hl: 1 + 2 * hl]
                qcell = self.rowS[0:1, 8 + hl: 9 + hl]
                mcell = self.rowS[0:1, 10 + hl: 11 + hl]
                E("DVE", OP("tensor_tensor", mcell, kcell, qcell, op=ALU.mult),
                  reads=["rowS"], writes=["rowS"])
                E("DVE", OP("tensor_scalar_add", mcell, mcell, 1.0e-30),
                  reads=["rowS"], writes=["rowS"])
                E("ACT", OP("activation", mcell, mcell, AF.Ln),
                  reads=["rowS"], writes=["rowS"])
                E("ACT", OP("activation", mcell, mcell, AF.Exp, scale=0.5),
                  reads=["rowS"], writes=["rowS"])
            for hl in range(2):
                DMA("SP", "slp", self.rowSLP[0:1, :],
                    d["alibik"][L, 2 * p + hl, 3:4], writes=["rowSLP"])
                mcell = self.rowS[0:1, 10 + hl: 11 + hl]
                for w in range(NW):
                    rw = self.rowA if w % 2 == 0 else self.rowB
                    rk = "rowA" if w % 2 == 0 else "rowB"
                    E("DVE", OP("tensor_single_scalar", rw[0:1, :],
                                self.rowSLP[0:1, w * WQ:(w + 1) * WQ], mcell, op=ALU.add),
                      reads=["rowSLP", "rowS"], writes=[rk])
                    E("DVE", OP("tensor_scalar_mul", pC[64:65, QAx[hl] + w * WQ: QAx[hl] + (w + 1) * WQ],
                                rw[0:1, :], -1.0),
                      reads=[rk], writes=[f"qaug{hl}"])
            for tt in range(16):
                bv = 6 + (tt % 2)
                self.mm_group(f"ps{bv}", ps[bv][:, 0:128],
                              [(xn[:, dc * 2048 + tt * 128: dc * 2048 + (tt + 1) * 128],
                                pB[:, base + 2048 + dc * 128: base + 2048 + (dc + 1) * 128])
                               for dc in range(NDC)],
                              reads=[wkq] + [f"xn_{dc}_{tt // 4}" for dc in range(NDC)])
                vt = VP + tt * 132
                E("DVE", OP("tensor_tensor", pC[:, vt:vt + 64], ps[bv][:, 0:64], sc[3][:, 0:64], op=ALU.add),
                  reads=[f"ps{bv}", "sc3"], writes=["vp"])
                E("DVE", OP("tensor_tensor", pC[:, vt + 65:vt + 129], ps[bv][:, 64:128], sc[3][:, 64:128], op=ALU.add),
                  reads=[f"ps{bv}", "sc3"], writes=["vp"])
            for w in range(NW):
                nkt = 4 * (w + 1)
                for hl in range(2):
                    hh = p * 2 + hl
                    KA, QA = KAx[hl], QAx[hl]
                    psY = ps[2 + hl]
                    yk = f"ps{2 + hl}"
                    for step in range(nkt + DEPTH):
                        if step < nkt:
                            kt = step
                            bs = SB[(kt + hl * 2) % 4]
                            self.mm_group(f"ps{bs}", ps[bs][:, :],
                                          [(pC[0:68, KA + kt * 128:KA + (kt + 1) * 128],
                                            pC[0:68, QA + w * WQ:QA + (w + 1) * WQ])],
                                          reads=[f"kaug{hl}", f"qaug{hl}"])
                            if kt >= 4 * w:
                                sub = kt - 4 * w
                                E("DVE", OP("tensor_tensor",
                                    ps[bs][:, sub * 128:(sub + 1) * 128], ps[bs][:, sub * 128:(sub + 1) * 128],
                                    self.cmaskt[:], op=ALU.add),
                                  reads=["cmaskt", f"ps{bs}"], writes=[f"ps{bs}"])

                            pti = (kt + hl * 2) % 4
                            ptb = PTb + pti * 512
                            sub0 = max(0, kt - 4 * w) if kt >= 4 * w else 0
                            if sub0 > 0:
                                E("DVE", OP("memset", pC[:, ptb:ptb + sub0 * 128], 0.0),
                                  writes=[f"pt{pti}"])
                            E("ACT", OP("activation", pC[:, ptb + sub0 * 128:ptb + 512],
                                        ps[bs][:, sub0 * 128:512], AF.Exp),
                              reads=[f"ps{bs}"], writes=[f"pt{pti}"])
                        if step >= DEPTH:
                            kt2 = step - DEPTH
                            pti2 = (kt2 + hl * 2) % 4
                            ptb2 = PTb + pti2 * 512
                            self.mm_group(yk, psY[0:65, :],
                                          [(pC[:, VP + kt2 * 132 + hl * 65: VP + kt2 * 132 + hl * 65 + 65],
                                            pC[:, ptb2:ptb2 + 512])],
                                          reads=["vp", f"pt{pti2}"], acc_cont=(kt2 > 0),
                                          group_end=(kt2 == nkt - 1))
                for hl in range(2):
                    hh = p * 2 + hl
                    psY = ps[2 + hl]
                    yk = f"ps{2 + hl}"
                    E("ACT", OP("activation", self.rowM[0:1, :], psY[64:65, :], AF.Ln),
                      reads=[yk], writes=["rowM"])
                    E("ACT", OP("activation", self.rowAb[0:1, :], self.rowM[0:1, :], AF.Exp, scale=-1.0),
                      reads=["rowM"], writes=["rowAb"])
                    self.mm_group("ps6", ps[6][0:64, :], [(self.onesRb[0:1, 0:64], self.rowAb[0:1, :])],
                                  reads=["rowAb", "onesRb"])
                    E("DVE", OP("tensor_copy", sc[2][0:64, :], ps[6][0:64, :]),
                      reads=["ps6"], writes=["sc2"])
                    ybase = (hh // 2) * 2048 + w * WQ
                    yrow = (hh % 2) * 64
                    E("DVE", OP("tensor_tensor",
                        pA[yrow:yrow + 64, ybase:ybase + WQ], psY[0:64, :], sc[2][0:64, :], op=ALU.mult),
                      reads=[yk, "sc2"], writes=[f"yT_{hh // 2}_{w}"])
        if "dbg_yT" in self.dbg_d and self.stop_after == "C":
            DMA("POOL", "dbgyT", self.dbg_d["dbg_yT"][b], pA[:],
                reads=[f"yT_{c}_{w}" for c in range(NDC) for w in range(NW)])
        em.fence(["pBx"])
        em.dmag("POOL", "ow",
                [(pB[:, dc * 1024:(dc + 1) * 1024], owsrc[dc * 128:(dc + 1) * 128, :])
                 for dc in range(NDC)], writes=["ow"])
        for w in range(NW):
            for dt in range(NDC):
                xb_key = "sc0" if dt % 2 == 0 else "sc3"
                xbuf = sc[0] if dt % 2 == 0 else sc[3]
                if L == 0:
                    DMA("SP", f"xrd{dt % 2}", xbuf[:], self.x_d[b, dt * 128:(dt + 1) * 128, w * WQ:(w + 1) * WQ],
                        writes=[xb_key])
                else:
                    DMA("SP", f"x1rd{dt % 2}", xbuf[:], self.x1scr[b, dt * 128:(dt + 1) * 128, w * WQ:(w + 1) * WQ],
                        reads=[f"x1_{dt}_{w}"], writes=[xb_key])
                bo = dt % 2
                self.mm_group(f"ps{bo}", ps[bo][:, :],
                              [(pB[:, dc * 1024 + dt * 128: dc * 1024 + (dt + 1) * 128],
                                pA[:, dc * 2048 + w * WQ: dc * 2048 + (w + 1) * WQ])
                               for dc in range(NDC)],
                              reads=["ow"] + [f"yT_{dc}_{w}" for dc in range(NDC)])
                asl = (slice(None), slice(dt * 2048 + w * WQ, dt * 2048 + (w + 1) * WQ))
                E("ACT", OP("activation", sc[1][:], acc[asl], AF.Silu),
                  reads=[self.acck(dt, w)], writes=["sc1"])
                obcol = self.bias(ob, dt)
                E("DVE", OP("scalar_tensor_tensor", sc[2][:], ps[bo][:, :], obcol, sc[1][:],
                            op0=ALU.add, op1=ALU.mult),
                  reads=[f"ps{bo}", "bfm", "sc1"], writes=["sc2"])
                E("DVE", OP("tensor_tensor", xbuf[:], xbuf[:], sc[2][:], op=ALU.add),
                  reads=[xb_key, "sc2"], writes=[xb_key])
                if L == 0:
                    DMA("SP", f"x1wr{dt % 2}", self.x1scr[b, dt * 128:(dt + 1) * 128, w * WQ:(w + 1) * WQ],
                        xbuf[:], reads=[xb_key], writes=[f"x1_{dt}_{w}"])
                    E("ACT", OP("square", tb[dt % 2][:], xbuf[:]), reads=[xb_key], writes=[f"tb{dt % 2}"])
                    E("DVE", OP("tensor_copy", sc[2][:].bitcast(BF16)[:, 0:512], xbuf[:]),
                      reads=[xb_key, "sc2"], writes=["sc2"])
                    self.mm_group("ps6", ps[6][0:1, :], [(self.onesCb[:], sc[2][:].bitcast(BF16)[:, 0:512])],
                                  reads=["sc2", "onesCb"], acc_cont=(dt > 0),
                                  group_end=(dt == NDC - 1))
                    self.mm_group("ps7", ps[7][0:1, :], [(self.onesCb[:], tb[dt % 2][:])],
                                  reads=[f"tb{dt % 2}", "onesCb"], acc_cont=(dt > 0),
                                  group_end=(dt == NDC - 1))
                else:
                    DMA("SP", f"outwr{dt % 2}", self.out_d[b, dt * 128:(dt + 1) * 128, w * WQ:(w + 1) * WQ],
                        xbuf[:], reads=[xb_key], writes=[f"outw{dt % 2}"])
            if L == 0:
                E("DVE", OP("tensor_scalar_mul", self.rowAb[:], ps[6][0:1, :], 1.0 / D),
                  reads=["ps6"], writes=["rowAb"])
                E("DVE", OP("tensor_scalar_mul", self.rowB[:], ps[7][0:1, :], 1.0 / D),
                  reads=["ps7"], writes=["rowB"])
                E("DVE", OP("tensor_tensor", self.rowM[:], self.rowAb[:], self.rowAb[:], op=ALU.mult),
                  reads=["rowAb"], writes=["rowM"])
                E("DVE", OP("tensor_tensor", self.rowB[:], self.rowB[:], self.rowM[:], op=ALU.subtract),
                  reads=["rowB", "rowM"], writes=["rowB"])
                E("DVE", OP("tensor_scalar_add", self.rowB[:], self.rowB[:], 1.0e-5),
                  reads=["rowB"], writes=["rowB"])
                E("ACT", OP("activation", self.rowM[:], self.rowB[:], AF.Sqrt),
                  reads=["rowB"], writes=["rowM"])
                E("DVE", OP("reciprocal", self.rowBb[:], self.rowM[:]),
                  reads=["rowM"], writes=["rowBb"])
                self.mm_group("ps4", ps[4][:, :], [(self.onesRb[:], self.rowAb[:])],
                              reads=["rowAb", "onesRb"])
                self.mm_group("ps5", ps[5][:, :], [(self.onesRb[:], self.rowBb[:])],
                              reads=["rowBb", "onesRb"])
                for dt in range(NDC):
                    xb_key = "sc0" if dt % 2 == 0 else "sc3"
                    xbuf = sc[0] if dt % 2 == 0 else sc[3]
                    DMA("SP", f"x1rd{dt % 2}", xbuf[:], self.x1scr[b, dt * 128:(dt + 1) * 128, w * WQ:(w + 1) * WQ],
                        reads=[f"x1_{dt}_{w}"], writes=[xb_key])
                    E("DVE", OP("tensor_tensor", sc[2][:], xbuf[:], ps[4][:, :], op=ALU.subtract),
                      reads=[xb_key, "ps4"], writes=["sc2"])
                    E("DVE", OP("tensor_tensor",
                        self.xn[:, dt * 2048 + w * WQ: dt * 2048 + (w + 1) * WQ], sc[2][:], ps[5][:, :], op=ALU.mult),
                      reads=["sc2", "ps5"], writes=[f"xn_{dt}_{w}"])
        if L == 0 and self.stop_after == "C" and "dbg_xn" in self.dbg_d:
            em.dma("POOL", "dbgxn", self.dbg_d["dbg_xn"][b], self.xn[:],
                   reads=[f"xn_{dc}_{w}" for dc in range(NDC) for w in range(NW)])
        if L == 0 and self.stop_after == "C" and "dbg_x1" in self.dbg_d:
            for dt in range(NDC):
                for w in range(NW):
                    DMA("SP", "dbgx1", self.dbg_d["dbg_x1"][b, :, dt * 2048 + w * WQ: dt * 2048 + (w + 1) * WQ],
                        self.x1scr[b, dt * 128:(dt + 1) * 128, w * WQ:(w + 1) * WQ],
                        reads=[f"x1_{dt}_{w}"])

    def phase_D(self, b):
        em, E, DMA = self.em, self.em.emit, self.em.dma
        em.fence(["pAx", "pBx", "pCx"])
        ps, sc, tb, acc, xn, pA, pB, pC = self.ps, self.sc, self.tb, self.acc, self.xn, self.pA, self.pB, self.pC
        d = self.d_in
        first = True
        for e in range(8):
            for jj in range(4):
                for j2 in range(2):
                    j = jj * 2 + j2
                    gk, uk = f"wg{j2}", f"wu{j2}"
                    em.dmag("POOL", gk,
                            [(pA[:, j2 * 4096 + dc * 512: j2 * 4096 + (dc + 1) * 512],
                              d["l1_wg"][e, dc * 128:(dc + 1) * 128, j * 512:(j + 1) * 512])
                             for dc in range(NDC)], writes=[gk])
                    em.dmag("POOL", uk,
                            [(pA[:, 8192 + j2 * 4096 + dc * 512: 8192 + j2 * 4096 + (dc + 1) * 512],
                              d["l1_wu"][e, dc * 128:(dc + 1) * 128, j * 512:(j + 1) * 512])
                             for dc in range(NDC)], writes=[uk])
                    wds = j % 3
                    em.dmag("POOL", f"wd{wds}",
                            [(pC[:, wds * 4096 + fc * 1024: wds * 4096 + (fc + 1) * 1024],
                              d["l1_wd"][e, j * 512 + fc * 128: j * 512 + (fc + 1) * 128, :])
                             for fc in range(4)], writes=[f"wd{wds}"])
                for w in range(NW):
                    for j2 in range(2):
                        gk, uk = f"wg{j2}", f"wu{j2}"
                        for u in range(4):
                            bx = 2 * (u % 2)
                            self.mm_group(f"ps{bx}", ps[bx][:, :],
                                          [(pA[:, j2 * 4096 + dc * 512 + u * 128: j2 * 4096 + dc * 512 + (u + 1) * 128],
                                            xn[:, dc * 2048 + w * WQ: dc * 2048 + (w + 1) * WQ])
                                           for dc in range(NDC)],
                                          reads=[gk] + [f"xn_{dc}_{w}" for dc in range(NDC)])
                            self.mm_group(f"ps{bx + 1}", ps[bx + 1][:, :],
                                          [(pA[:, 8192 + j2 * 4096 + dc * 512 + u * 128: 8192 + j2 * 4096 + dc * 512 + (u + 1) * 128],
                                            xn[:, dc * 2048 + w * WQ: dc * 2048 + (w + 1) * WQ])
                                           for dc in range(NDC)],
                                          reads=[uk] + [f"xn_{dc}_{w}" for dc in range(NDC)])
                            E("ACT", OP("activation", tb[u % 2][:], ps[bx][:, :], AF.Silu),
                              reads=[f"ps{bx}"], writes=[f"tb{u % 2}"])
                            mb = pB
                            mo = (w % 2) * 4096
                            mc = mo + (j2 * 4 + u) * 512
                            E("DVE", OP("tensor_tensor",
                                mb[:, mc:mc + 512], ps[bx + 1][:, :], tb[u % 2][:], op=ALU.mult),
                              reads=[f"ps{bx + 1}", f"tb{u % 2}"], writes=[f"mbuf{w % 2}"])
                    for dt in range(NDC):
                        by = 4 + (dt % 2)
                        mb = pB
                        mo = (w % 2) * 4096
                        s0, s1 = (jj * 2) % 3, (jj * 2 + 1) % 3
                        self.mm_group(f"ps{by}", ps[by][:, :],
                                      [(pC[:, (s0 if fc < 4 else s1) * 4096 + (fc % 4) * 1024 + dt * 128:
                                            (s0 if fc < 4 else s1) * 4096 + (fc % 4) * 1024 + (dt + 1) * 128],
                                        mb[:, mo + fc * 512: mo + (fc + 1) * 512])
                                       for fc in range(8)],
                                      reads=[f"wd{s0}", f"wd{s1}", f"mbuf{w % 2}"])
                        asl = (slice(None), slice(dt * 2048 + w * WQ, dt * 2048 + (w + 1) * WQ))
                        if first and jj == 0:
                            E("DVE", OP("tensor_copy", acc[asl], ps[by][:, :]),
                              reads=[f"ps{by}"], writes=[self.acck(dt, w)])
                        else:
                            E("DVE", OP("tensor_tensor", 
                                acc[asl], acc[asl], ps[by][:, :], op=ALU.add),
                              reads=[f"ps{by}", self.acck(dt, w)], writes=[self.acck(dt, w)])
            first = False
        if self.stop_after == "D" and "dbg_acc" in self.dbg_d:
            DMA("SP", "dbgacc", self.dbg_d["dbg_acc"][b], acc[:],
                reads=[self.acck(dt, w) for dt in range(NDC) for w in range(NW)])


def run(builder, shared, x_core, core_ids=(0,), trace=False):
    im = dict(shared)
    im["x"] = x_core
    in_maps = [im] * len(core_ids)
    return run_bass_kernel_spmd(builder.nc, in_maps, core_ids=list(core_ids), trace=trace)



def kernel(**inputs):
    shared, per_core_x = host_prepare(inputs)
    b = Builder(n_batches=2).build()
    in_maps = []
    for c in range(8):
        im = dict(shared)
        im["x"] = per_core_x[c]
        in_maps.append(im)
    res = run_bass_kernel_spmd(b.nc, in_maps, core_ids=list(range(8)))
    outs = [np.asarray(r["out"]) for r in res.results]
    full = np.concatenate([o.transpose(0, 2, 1) for o in outs], axis=0)
    return np.ascontiguousarray(full.astype(np.float32))
